# version 21
# baseline (speedup 1.0000x reference)
import numpy as np
import ml_dtypes
from contextlib import ExitStack

import concourse.bass as bass
import concourse.mybir as mybir
from concourse.bass_utils import run_bass_kernel_spmd

F32 = mybir.dt.float32
BF16 = mybir.dt.bfloat16
AF = mybir.ActivationFunctionType
ALU = mybir.AluOpType
AX = mybir.AxisListType

S = 2048
HALO = 3
SH = S + HALO
E = 2048
NKT = 16
PROJ = 9280
INTER = 4096
CDIM = 5120
FFH = 5632
NFT = 44
NH = 64
CH = 256
NCHUNK = 8
LN_EPS = 1e-5
RMS_EPS = 1e-6
NW = FFH * E

DBG = False
LAST_STAGE = 9


class Sem:
    def __init__(self, stk, nc, name):
        self.h = stk.enter_context(nc.semaphore(name))
        self.n = 0

    def inc(self, k=1):
        self.n += k
        return self.n


class Chain:

    def __init__(self, eng, sem, rec):
        self.e = eng
        self.s = sem
        self.rec = rec

    def wait(self, h, v):
        if v > 0:
            self.e.wait_ge(h, v)

    def op(self, name, *a, **kw):
        self.e.wait_ge(self.s.h, self.s.n)
        getattr(self.e, name)(*a, **kw).then_inc(self.s.h, 1)
        return self.s.inc()

    def ck(self, key):
        self.rec[key] = self.s.n


class Ctx:
    pass


def build_nc(last_stage=None):
    if last_stage is None:
        last_stage = LAST_STAGE
    rec = {}
    _build(last_stage, {}, rec)
    rec2 = {}
    nc = _build(last_stage, rec, rec2)
    return nc


def _build(last_stage, prev, rec):
    nc = bass.Bass("TRN2", target_bir_lowering=False)
    scrkind = "ExternalOutput" if DBG else "Internal"
    X = Ctx()
    X.nc = nc
    X.last_stage = last_stage
    X.out_written = False
    X.final = []
    X.rec = rec
    X.prev = prev
    X.ck = lambda key: prev.get(key, 0)

    X.x_d = nc.dram_tensor("x", [SH, E], F32, kind="ExternalInput").ap()
    X.win_d = nc.dram_tensor("w_in", [73, 128, NKT * 128], F32, kind="ExternalInput").ap()
    X.convw_d = nc.dram_tensor("conv_w", [128, 160], F32, kind="ExternalInput").ap()
    X.convb_d = nc.dram_tensor("conv_b", [128, 40], F32, kind="ExternalInput").ap()
    X.dtb_d = nc.dram_tensor("dt_bias", [NH, 1], F32, kind="ExternalInput").ap()
    X.alog_d = nc.dram_tensor("a_log", [NH, 1], F32, kind="ExternalInput").ap()
    X.drep_d = nc.dram_tensor("d_rep", [128, 32], F32, kind="ExternalInput").ap()
    X.normw_d = nc.dram_tensor("norm_w", [128, 32], F32, kind="ExternalInput").ap()
    X.wout_d = nc.dram_tensor("w_out", [128, 32 * E], F32, kind="ExternalInput").ap()
    X.w1_d = nc.dram_tensor("w1", [E, FFH], F32, kind="ExternalInput").ap()
    X.w2_d = nc.dram_tensor("w2", [E, FFH], F32, kind="ExternalInput").ap()
    X.w3_d = nc.dram_tensor("w3", [FFH, E], F32, kind="ExternalInput").ap()
    X.identb_d = nc.dram_tensor("identb", [128, 128], BF16, kind="ExternalInput").ap()
    X.identf_d = nc.dram_tensor("identf", [128, 128], F32, kind="ExternalInput").ap()
    X.mask_d = nc.dram_tensor("mask", [256, 256], F32, kind="ExternalInput").ap()
    X.y_d = nc.dram_tensor("y", [S, E], F32, kind="ExternalOutput").ap()

    X.scr_acum = nc.dram_tensor("scr_acum", [NH, S], F32, kind=scrkind).ap()
    X.scr_sd = nc.dram_tensor("scr_sd", [NH, S], BF16, kind=scrkind).ap()
    X.scr_dtd = nc.dram_tensor("scr_dtd", [NH, S], BF16, kind=scrkind).ap()
    X.scr_b = nc.dram_tensor("scr_b", [512, S], BF16, kind=scrkind).ap()
    X.scr_c = nc.dram_tensor("scr_c", [512, S], BF16, kind=scrkind).ap()
    X.scr_yg = nc.dram_tensor("scr_yg", [INTER, S], BF16, kind=scrkind).ap()
    X.scr_out1 = nc.dram_tensor("scr_out1", [S, E], BF16, kind=scrkind).ap()
    X.scr_x2t = nc.dram_tensor("scr_x2t", [E, S], BF16, kind=scrkind).ap()
    X.scr_q3 = nc.dram_tensor("scr_q3", [FFH, E], BF16, kind=scrkind).ap()
    X.scr_hid = nc.dram_tensor("scr_hid", [FFH, S], BF16, kind=scrkind).ap()
    X.scr_row = nc.dram_tensor("scr_row", [1, E], F32, kind=scrkind).ap()
    X.scr_g = nc.dram_tensor("scr_g", [1, 8], F32, kind=scrkind).ap()
    X.scr_gt = nc.dram_tensor("scr_gt", [1, 128], F32, kind=scrkind).ap()

    with ExitStack() as glob, nc.Block() as block:
        X.block = block
        X.glob = glob
        X.sem = lambda name: Sem(glob, nc, name)
        gsb = lambda name, shp, dt: glob.enter_context(nc.sbuf_tensor(name, shp, dt))

        X.vs = X.sem("vself")
        X.ss = X.sem("sself")
        X.VC = lambda v: Chain(v, X.vs, rec)
        X.SC = lambda sc: Chain(sc, X.ss, rec)

        X.identb = gsb("identb_s", [128, 128], BF16)
        X.identf = gsb("identf_s", [128, 128], F32)
        X.mask_sb = gsb("mask_sb", [128, 512], F32)
        X.convw_sb = gsb("convw_sb", [128, 160], F32)
        X.convb_sb = gsb("convb_sb", [128, 40], F32)
        X.hconvb_sb = gsb("hconvb_sb", [128, 40], F32)
        X.dtb_sb = gsb("dtb_sb", [NH, 1], F32)
        X.alog_sb = gsb("alog_sb", [NH, 1], F32)
        X.dcol_sb = gsb("dcol_sb", [128, 32], F32)
        X.normw_sb = gsb("normw_sb", [128, 32], F32)
        X.ones_sb = gsb("ones_sb", [128, 1], F32)
        X.fconst = gsb("fconst", [128, 2], F32)
        X.parts = gsb("parts", [128, 52], F32)
        X.gcol = gsb("gcol", [128, 4], F32)
        X.tpos = gsb("tpos", [128, 8], F32)
        X.acc = gsb("acc", [128, S], F32)

        X.s_c = X.sem("s_c")

        @block.sync
        def _(sy):
            for dst, src in [
                (X.identb[:], X.identb_d), (X.identf[:], X.identf_d),
                (X.mask_sb[:, 0:256], X.mask_d[0:128, :]),
                (X.mask_sb[:, 256:512], X.mask_d[128:256, :]),
                (X.convw_sb[:], X.convw_d), (X.convb_sb[:], X.convb_d),
                (X.dtb_sb[:], X.dtb_d), (X.alog_sb[:], X.alog_d),
                (X.dcol_sb[:], X.drep_d), (X.normw_sb[:], X.normw_d),
            ]:
                sy.dma_start(out=dst, in_=src).then_inc(X.s_c.h, 16)
                X.s_c.inc(16)

        @block.vector
        def _(v):
            V = X.VC(v)
            V.wait(X.s_c.h, X.s_c.n)
            V.op("tensor_scalar_mul", out=X.hconvb_sb[:], in0=X.convb_sb[:], scalar1=0.5)
            V.op("memset", X.acc[:], 0.0)
            V.op("memset", X.ones_sb[:], 1.0)
            V.op("memset", X.fconst[:, 0:1], LN_EPS)
            V.op("memset", X.fconst[:, 1:2], RMS_EPS)
            V.ck("misc")

        with ExitStack() as att:
            X.asb = lambda name, shp, dt: att.enter_context(nc.sbuf_tensor(name, shp, dt))
            X.aps = lambda name, shp, dt: att.enter_context(nc.psum_tensor(name, shp, dt))
            _attn_base(X)
            with ExitStack() as s01:
                X.bsb = lambda name, shp, dt: s01.enter_context(nc.sbuf_tensor(name, shp, dt))
                _stage0(X)
                if last_stage >= 1:
                    _stage1(X)
            if last_stage >= 2:
                _stage2(X)

        if last_stage >= 3:
            with ExitStack() as st3:
                X.csb = lambda name, shp, dt: st3.enter_context(nc.sbuf_tensor(name, shp, dt))
                X.cps = lambda name, shp, dt: st3.enter_context(nc.psum_tensor(name, shp, dt))
                _stage3(X)
        if last_stage >= 4:
            _stage4(X)

        _finish(X)

    return nc


def _attn_base(X):
    asb, aps = X.asb, X.aps
    X.xlnT = asb("xlnT", [128, NKT * SH], BF16)
    X.dsc = asb("dsc", [128, 2048], F32)
    X.wbuf = [asb(f"wbuf{i}", [128, NKT * 128], BF16) for i in range(2)]
    X.wgbuf = [asb("wgbuf0", [128, NKT * 128], BF16)]
    X.hpre = asb("hpre", [128, SH], BF16)
    X.cpre = asb("cpre", [128, S], BF16)
    X.thb = asb("thb", [128, S], BF16)
    X.biasDT = asb("biasDT", [128, 1024], F32)
    X.diag4 = asb("diag4", [128, 512], BF16)
    X.cols = asb("cols", [128, 16], F32)

    X.ps_a = aps("ps_a", [128, 1024], F32)
    X.ps_b = aps("ps_b", [128, 1024], F32)
    X.ps_h = aps("ps_h", [128, 512], F32)
    X.ps_t = aps("ps_t", [128, 1024], BF16)
    X.ps_yd = aps("ps_yd", [128, 512], F32)
    X.ps_g = aps("ps_g", [128, 512], F32)

    X.xk = lambda k, a, b: X.xlnT[:, k * SH + a: k * SH + b]


def _stage0(X):
    block = X.block
    ck = X.ck
    xs = [X.bsb(f"xs{i}", [128, E], F32) for i in range(2)]
    xlb = [X.bsb(f"xlb{i}", [128, E], BF16) for i in range(2)]
    cols = X.cols
    dsc = X.dsc

    s_x = [X.sem("s_x0"), X.sem("s_x1")]
    s_tp = X.sem("s_tp")

    strips = [(3 + 128 * i, 128, 3 + 128 * i) for i in range(16)] + [(0, HALO, 0)]
    X.NS0 = len(strips)

    @block.sync
    def _(sy):
        for i, (r0, nr, c0) in enumerate(strips):
            if i >= 2:
                sy.wait_ge(X.vs.h, ck(f"xl{i-2}"))
                sy.wait_ge(X.ss.h, ck(f"sq{i-2}"))
            sy.dma_start(out=xs[i % 2][0:nr, :], in_=X.x_d[r0:r0 + nr, :]).then_inc(s_x[i % 2].h, 16)
            s_x[i % 2].inc(16)

    @block.scalar
    def _(sc):
        S_ = X.SC(sc)
        S_.wait(X.vs.h, ck("misc"))
        for i, (r0, nr, c0) in enumerate(strips):
            w = 128 if nr == 128 else 4
            ww = 128 if nr == 128 else HALO
            S_.wait(s_x[i % 2].h, 16 * (i // 2 + 1))
            if i >= 1:
                S_.wait(X.vs.h, ck(f"st{i-1}"))
            S_.op("activation", out=dsc[0:nr, :], in_=xs[i % 2][0:nr, :], func=AF.Square,
                  accum_out=cols[0:nr, 1:2])
            S_.ck(f"sq{i}")
            S_.wait(X.vs.h, ck(f"st{i}"))
            S_.op("activation", out=cols[0:nr, 6:7], in_=cols[0:nr, 5:6], func=AF.Ln,
                  bias=X.fconst[0:nr, 0:1], scale=1.0)
            S_.op("activation", out=cols[0:nr, 7:8], in_=cols[0:nr, 6:7], func=AF.Exp,
                  scale=-0.5)
            S_.ck(f"rs{i}")
            for half in range(2):
                bidx = 2 * i + half
                S_.wait(s_tp.h, bidx + 1)
                src = X.ps_t[:, 0:8 * w].rearrange("p (d s) -> p d s", s=w)[:, :, 0:ww]
                dst = X.xlnT[:].rearrange("p (k s) -> p k s", s=SH)[:, half * 8:half * 8 + 8, c0:c0 + ww]
                S_.op("copy", out=dst, in_=src)
                S_.ck(f"te0_{bidx}")

    @block.vector
    def _(v):
        V = X.VC(v)
        for i, (r0, nr, c0) in enumerate(strips):
            V.wait(s_x[i % 2].h, 16 * (i // 2 + 1))
            V.op("reduce_sum", out=cols[0:nr, 0:1], in_=xs[i % 2][0:nr, :], axis=AX.X)
            V.wait(X.ss.h, ck(f"sq{i}"))
            V.op("tensor_scalar_mul", out=cols[0:nr, 2:3], in0=cols[0:nr, 0:1], scalar1=1.0 / E)
            V.op("tensor_scalar_mul", out=cols[0:nr, 3:4], in0=cols[0:nr, 1:2], scalar1=1.0 / E)
            V.op("tensor_mul", out=cols[0:nr, 4:5], in0=cols[0:nr, 2:3], in1=cols[0:nr, 2:3])
            V.op("tensor_sub", out=cols[0:nr, 5:6], in0=cols[0:nr, 3:4], in1=cols[0:nr, 4:5])
            V.ck(f"st{i}")
            V.wait(X.ss.h, ck(f"rs{i}"))
            V.op("scalar_tensor_tensor", out=cols[0:nr, 8:9], in0=cols[0:nr, 2:3], scalar=-1.0,
                 in1=cols[0:nr, 7:8], op0=ALU.mult, op1=ALU.mult)
            if i >= 2:
                V.wait(s_tp.h, 2 * (i - 1))
            V.op("tensor_scalar", out=xlb[i % 2][0:nr, :], in0=xs[i % 2][0:nr, :],
                 scalar1=cols[0:nr, 7:8], scalar2=cols[0:nr, 8:9],
                 op0=ALU.mult, op1=ALU.add)
            V.ck(f"xl{i}")

    @block.tensor
    def _(t):
        t.wait_ge(X.s_c.h, X.s_c.n)
        for i, (r0, nr, c0) in enumerate(strips):
            w = 128 if nr == 128 else 4
            ww = 128 if nr == 128 else HALO
            t.wait_ge(X.vs.h, ck(f"xl{i}"))
            for half in range(2):
                bidx = 2 * i + half
                if bidx >= 1:
                    t.wait_ge(X.ss.h, ck(f"te0_{bidx-1}"))
                for d in range(8):
                    dt_i = half * 8 + d
                    ti = t.transpose(out=X.ps_t[:, d * w:d * w + ww],
                                     in_=xlb[i % 2][0:nr, dt_i * 128:(dt_i + 1) * 128],
                                     identity=X.identb[0:nr, 0:nr])
                ti.then_inc(s_tp.h, 1)
                s_tp.inc()

    X.final.append(("ck", X.ss, "te0_33"))


def _stage1(X):
    block = X.block
    ck = X.ck
    xk = X.xk
    wbuf = X.wbuf; hpre = X.hpre; cpre = X.cpre; thb = X.thb
    ps_a, ps_b, ps_h = X.ps_a, X.ps_b, X.ps_h

    silb = X.bsb("silb", [128, S], BF16)

    s_wl = [X.sem("s_wl0"), X.sem("s_wl1")]
    s_mm = X.sem("s_mm")
    s_cvm = X.sem("s_cvm")
    s_sp = X.sem("s_sp")
    s_at = X.sem("s_at")
    X.s_sp = s_sp
    X.s_mm1 = s_mm

    tiles1 = list(range(64, 72))

    @block.gpsimd
    def _(g):
        g.wait_ge(X.ss.h, ck("te0_33"))
        for j, tt in enumerate(tiles1 + [72]):
            if j >= 2:
                g.wait_ge(s_mm.h, j - 1)
            w = 64 if tt == 72 else 128
            g.dma_start(out=X.wbuf[j % 2][:, 0:NKT * w].rearrange("p (k m) -> p k m", k=NKT),
                        in_=X.win_d[tt, :, :].rearrange("p (k m) -> p k m", k=NKT)[:, :, 0:w]
                        ).then_inc(s_wl[j % 2].h, 16)
            s_wl[j % 2].inc(16)

    @block.tensor
    def _(t):
        for j, tt in enumerate(tiles1 + [72]):
            w = 64 if tt == 72 else 128
            t.wait_ge(s_wl[j % 2].h, 16 * (j // 2 + 1))
            if 1 <= j <= 8:
                t.wait_ge(X.ss.h, ck(f"cve1_{j-1}"))
            lhs = lambda k: wbuf[j % 2][:, k * w:(k + 1) * w]
            mi = None
            for half, ps in ((0, ps_a), (1, ps_b)):
                for nb in range(2):
                    for k in range(NKT):
                        mi = t.matmul(out=ps[0:w, nb * 512:(nb + 1) * 512], lhsT=lhs(k),
                                      rhs=xk(k, 3 + half * 1024 + nb * 512, 3 + half * 1024 + (nb + 1) * 512),
                                      start=(k == 0), stop=(k == NKT - 1))
            if tt != 72:
                for k in range(NKT):
                    mi = t.matmul(out=ps_h[0:w, 0:HALO], lhsT=lhs(k), rhs=xk(k, 0, HALO),
                                  start=(k == 0), stop=(k == NKT - 1))
            mi.then_inc(s_mm.h, 1)
            s_mm.inc()
            if tt != 72:
                t.wait_ge(X.ss.h, ck(f"ev1_{j}"))
                t.wait_ge(X.vs.h, ck(f"dg1_{j}"))
                if j >= 1:
                    t.wait_ge(X.ss.h, ck(f"cve1_{j-1}"))
                for half in range(2):
                    banks = (X.ps_yd, X.ps_g) if half == 0 else (ps_a, ps_b)
                    mi = None
                    for nbl, psx in ((0, banks[0]), (1, banks[1])):
                        b0 = half * 1024 + nbl * 512
                        for k in range(4):
                            mi = t.matmul(out=psx[:, 0:512], lhsT=X.diag4[:, k * 128:(k + 1) * 128],
                                          rhs=hpre[:, b0 + k: b0 + k + 512],
                                          start=(k == 0), stop=(k == 3))
                    mi.then_inc(s_cvm.h, 1)
                    s_cvm.inc()

    @block.scalar
    def _(sc):
        S_ = X.SC(sc)
        for j, tt in enumerate(tiles1):
            S_.wait(s_mm.h, j + 1)
            if j >= 1:
                S_.wait(s_cvm.h, 2 * j)
            S_.op("copy", out=hpre[:, 0:HALO], in_=ps_h[:, 0:HALO])
            S_.op("copy", out=hpre[:, HALO:HALO + 1024], in_=ps_a[:, 0:1024])
            S_.op("copy", out=hpre[:, HALO + 1024:SH], in_=ps_b[:, 0:1024])
            S_.ck(f"ev1_{j}")
            tc = tt - 32
            if j >= 1:
                S_.wait(X.vs.h, ck(f"si1_{j-1}"))
            S_.wait(s_cvm.h, 2 * j + 1)
            S_.op("copy", out=cpre[:, 0:512], in_=X.ps_yd[:, 0:512])
            S_.op("copy", out=cpre[:, 512:1024], in_=X.ps_g[:, 0:512])
            S_.wait(s_cvm.h, 2 * j + 2)
            S_.op("copy", out=cpre[:, 1024:1536], in_=ps_a[:, 0:512])
            S_.op("copy", out=cpre[:, 1536:2048], in_=ps_b[:, 0:512])
            S_.ck(f"cve1_{j}")
            S_.op("activation", out=thb[:], in_=cpre[:], func=AF.Tanh, scale=0.5,
                  bias=X.hconvb_sb[:, tc:tc + 1])
            S_.ck(f"th1_{j}")

    @block.vector
    def _(v):
        V = X.VC(v)
        for j, tt in enumerate(tiles1):
            tc = tt - 32
            if j >= 1:
                V.wait(s_cvm.h, 2 * j)
            for k in range(4):
                V.op("tensor_scalar_mul", out=X.diag4[:, k * 128:(k + 1) * 128],
                     in0=X.identb[:], scalar1=X.convw_sb[:, tc * 4 + k:tc * 4 + k + 1])
            V.ck(f"dg1_{j}")
            V.wait(X.ss.h, ck(f"th1_{j}"))
            if j >= 1:
                V.wait(s_sp.h, 16 * j)
            V.op("tensor_scalar", out=thb[:], in0=thb[:], scalar1=0.5, scalar2=0.5,
                 op0=ALU.mult, op1=ALU.add)
            V.op("scalar_tensor_tensor", out=silb[:], in0=cpre[:], scalar=X.convb_sb[:, tc:tc + 1],
                 in1=thb[:], op0=ALU.add, op1=ALU.mult)
            V.ck(f"si1_{j}")

    @block.sync
    def _(sy):
        for j, tt in enumerate(tiles1):
            sy.wait_ge(X.vs.h, ck(f"si1_{j}"))
            dst = X.scr_b if tt < 68 else X.scr_c
            r0 = ((tt - 64) if tt < 68 else (tt - 68)) * 128
            sy.dma_start(out=dst[r0:r0 + 128, :], in_=silb[:]).then_inc(s_sp.h, 16)
            s_sp.inc(16)

    u_t = X.bsb("u_t", [NH, S], F32)
    dt_t = X.bsb("dt_t", [NH, S], F32)
    lndt_t = X.bsb("lndt_t", [NH, S], F32)
    acum_t = X.bsb("acum_t", [NH, S], F32)
    tmpa = X.bsb("tmpa", [NH, S], F32)
    sdb = X.bsb("sdb", [NH, S], BF16)
    ddb = X.bsb("ddb", [NH, S], BF16)
    z64 = X.bsb("z64", [NH, CH], F32)
    acol = X.bsb("acol", [NH, 2], F32)

    @block.vector
    def _(v):
        V = X.VC(v)
        V.wait(s_mm.h, 9)
        V.op("memset", z64[:], 0.0)
        V.op("tensor_scalar_add", out=u_t[:, 0:1024], in0=ps_a[0:NH, 0:1024], scalar1=X.dtb_sb[:])
        V.op("tensor_scalar_add", out=u_t[:, 1024:2048], in0=ps_b[0:NH, 0:1024], scalar1=X.dtb_sb[:])
        V.ck("dtu")

    @block.scalar
    def _(sc):
        S_ = X.SC(sc)
        S_.wait(X.vs.h, ck("dtu"))
        S_.op("activation", out=tmpa[:], in_=u_t[:], func=AF.Abs)
        S_.op("activation", out=tmpa[:], in_=tmpa[:], func=AF.Exp, scale=-1.0)
        S_.ck("dtexp")

    @block.vector
    def _(v):
        V = X.VC(v)
        V.wait(X.ss.h, ck("dtexp"))
        V.op("tensor_scalar_add", out=tmpa[:], in0=tmpa[:], scalar1=1.0)
        V.ck("dtp1")

    @block.scalar
    def _(sc):
        S_ = X.SC(sc)
        S_.wait(X.vs.h, ck("dtp1"))
        S_.op("activation", out=tmpa[:], in_=tmpa[:], func=AF.Ln)
        S_.op("activation", out=acol[:, 0:1], in_=X.alog_sb[:], func=AF.Exp)
        S_.ck("dtln")

    @block.vector
    def _(v):
        V = X.VC(v)
        V.wait(X.ss.h, ck("dtln"))
        V.op("tensor_scalar_max", out=u_t[:], in0=u_t[:], scalar1=0.0)
        V.op("tensor_add", out=dt_t[:], in0=tmpa[:], in1=u_t[:])
        V.op("tensor_scalar_mul", out=acol[:, 1:2], in0=acol[:, 0:1], scalar1=-1.0)
        V.op("tensor_scalar_mul", out=tmpa[:], in0=dt_t[:], scalar1=acol[:, 1:2])
        for c in range(NCHUNK):
            V.op("tensor_tensor_scan", out=acum_t[:, c * CH:(c + 1) * CH],
                 data0=tmpa[:, c * CH:(c + 1) * CH], data1=z64[:],
                 initial=0.0, op0=ALU.add, op1=ALU.add)
        V.ck("acum")

    @block.scalar
    def _(sc):
        S_ = X.SC(sc)
        S_.wait(X.vs.h, ck("acum"))
        S_.op("activation", out=lndt_t[:], in_=dt_t[:], func=AF.Ln)
        S_.op("activation", out=sdb[:], in_=acum_t[:], func=AF.Exp)
        S_.ck("sdb")
        for c in range(NCHUNK):
            S_.op("activation", out=tmpa[:, c * CH:(c + 1) * CH], in_=acum_t[:, c * CH:(c + 1) * CH],
                  func=AF.Exp, scale=-1.0,
                  bias=acum_t[:, c * CH + CH - 1: c * CH + CH])
        S_.ck("decay")

    @block.vector
    def _(v):
        V = X.VC(v)
        V.wait(X.ss.h, ck("decay"))
        V.op("tensor_mul", out=ddb[:], in0=dt_t[:], in1=tmpa[:])
        V.ck("ddb")
        V.op("tensor_sub", out=tmpa[:], in0=lndt_t[:], in1=acum_t[:])
        V.ck("biasf")

    @block.tensor
    def _(t):
        t.wait_ge(X.vs.h, ck("biasf"))
        t.wait_ge(X.ss.h, ck("ev1_7"))
        for c in range(NCHUNK):
            if c >= 1:
                t.wait_ge(X.ss.h, ck(f"ae{c-1}"))
            ti = None
            for st in range(2):
                ti = t.transpose(out=ps_h[:, st * 64:(st + 1) * 64],
                                 in_=tmpa[:, c * CH + st * 128: c * CH + (st + 1) * 128],
                                 identity=X.identf[0:NH, 0:NH])
            ti.then_inc(s_at.h, 1)
            s_at.inc()

    @block.scalar
    def _(sc):
        S_ = X.SC(sc)
        for c in range(NCHUNK):
            S_.wait(s_at.h, c + 1)
            S_.op("copy", out=X.biasDT[:, c * 128:(c + 1) * 128], in_=ps_h[:, 0:128])
            S_.ck(f"ae{c}")

    @block.sync
    def _(sy):
        sy.wait_ge(X.ss.h, ck("sdb"))
        sy.dma_start(out=X.scr_sd[:], in_=sdb[:]).then_inc(s_sp.h, 16); s_sp.inc(16)
        sy.wait_ge(X.vs.h, ck("biasf"))
        sy.dma_start(out=X.scr_acum[:], in_=acum_t[:]).then_inc(s_sp.h, 16); s_sp.inc(16)
        sy.dma_start(out=X.scr_dtd[:], in_=ddb[:]).then_inc(s_sp.h, 16); s_sp.inc(16)

    X.final.append((s_sp, s_sp.n))


def _stage2(X):
    block = X.block
    ck = X.ck
    asb = X.asb
    xk = X.xk
    wbuf, wgbuf = X.wbuf, X.wgbuf
    hpre, cpre, thb = X.hpre, X.cpre, X.thb
    dsc = X.dsc
    ps_a, ps_b, ps_h, ps_t, ps_yd, ps_g = X.ps_a, X.ps_b, X.ps_h, X.ps_t, X.ps_yd, X.ps_g

    bsil = asb("bsil", [128, S], BF16)
    csil = asb("csil", [128, S], BF16)
    gcache = asb("gcache", [128, 8192], BF16)
    acbc = [asb(f"acbc{i}", [128, S], F32) for i in range(2)]
    sdp = [asb(f"sdp{i}", [128, S], BF16) for i in range(2)]
    ddp = [asb(f"ddp{i}", [128, S], BF16) for i in range(2)]
    mbuf = [asb(f"mbuf{i}", [128, 4096], BF16) for i in range(2)]
    bg = [mbuf[0][0:NH, 0:S], mbuf[1][0:NH, 0:S]]
    cg = [mbuf[0][0:NH, S:2 * S], mbuf[1][0:NH, S:2 * S]]
    htall = asb("htall", [128, 2048], BF16)
    hpair = asb("hpair", [128, S], BF16)
    gsb = asb("gsb2", [128, S], BF16)
    ypair = [asb(f"ypair{i}", [128, S], BF16) for i in range(2)]
    stcol = asb("stcol", [128, 8], F32)

    s2_bc = X.sem("s2_bc")
    s2_wh = [X.sem("s2_wh0"), X.sem("s2_wh1")]
    s2_wg = X.sem("s2_wg")
    s2_ab = [X.sem("s2_ab0"), X.sem("s2_ab1")]
    s2_sd = [X.sem("s2_sd0"), X.sem("s2_sd1")]
    s2_spl = [X.sem("s2_spl0"), X.sem("s2_spl1")]
    s2_mm = X.sem("s2_mm"); s2_gmm = X.sem("s2_gmm"); s2_cvm = X.sem("s2_cvm")
    s2_tp = X.sem("s2_tp"); s2_yd = X.sem("s2_yd"); s2_gq = X.sem("s2_gq")

    pairs = [ph + 4 * j for ph in range(4) for j in range(8)]
    NP = len(pairs)
    X.NP = NP
    X.s2_spl = s2_spl

    @block.gpsimd
    def _(g):
        g.wait_ge(X.s_mm1.h, 9)
        for idx, t_ in enumerate(pairs):
            if idx >= 2:
                g.wait_ge(s2_gmm.h, idx - 1)
            g.dma_start(out=wbuf[idx % 2][:],
                        in_=X.win_d[32 + t_, :, :]).then_inc(s2_wh[idx % 2].h, 16)
            g.wait_ge(s2_gmm.h, idx)
            g.dma_start(out=wgbuf[0][:],
                        in_=X.win_d[t_, :, :]).then_inc(s2_wg.h, 16)
            s2_wh[idx % 2].inc(16); s2_wg.inc(16)

    @block.sync
    def _(sy):
        sy.wait_ge(X.s_sp.h, X.s_sp.n)
        sy.wait_ge(X.ss.h, ck("ae7"))
        for idx, t_ in enumerate(pairs):
            ph = t_ % 4
            h1, h2 = 2 * t_, 2 * t_ + 1
            if idx % 8 == 0:
                if idx > 0:
                    sy.wait_ge(X.vs.h, ck(f"u{idx-1}"))
                    sy.wait_ge(s2_yd.h, 8 * idx)
                sy.dma_start(out=bsil[:], in_=X.scr_b[ph * 128:(ph + 1) * 128, :]).then_inc(s2_bc.h, 16)
                sy.dma_start(out=csil[:], in_=X.scr_c[ph * 128:(ph + 1) * 128, :]).then_inc(s2_bc.h, 16)
                sy.dma_start(out=bg[0], in_=X.scr_b[ph * 128:ph * 128 + 64, :]).then_inc(s2_bc.h, 16)
                sy.dma_start(out=bg[1], in_=X.scr_b[ph * 128 + 64:(ph + 1) * 128, :]).then_inc(s2_bc.h, 16)
                sy.dma_start(out=cg[0], in_=X.scr_c[ph * 128:ph * 128 + 64, :]).then_inc(s2_bc.h, 16)
                sy.dma_start(out=cg[1], in_=X.scr_c[ph * 128 + 64:(ph + 1) * 128, :]).then_inc(s2_bc.h, 16)
                s2_bc.inc(96)
            for hh in range(2):
                hidx = 2 * idx + hh
                hab = h1 + hh
                if hidx >= 2:
                    sy.wait_ge(X.vs.h, ck(f"d{hidx-2}"))
                sy.dma_start(out=acbc[hidx % 2][:],
                             in_=X.scr_acum[hab:hab + 1, :].partition_broadcast(128)).then_inc(s2_ab[hidx % 2].h, 16)
                s2_ab[hidx % 2].inc(16)
            if idx >= 2:
                sy.wait_ge(X.vs.h, ck(f"u{idx-2}"))
            sy.dma_start(out=sdp[idx % 2][0:64, :], in_=X.scr_sd[h1:h1 + 1, :].partition_broadcast(64)).then_inc(s2_sd[idx % 2].h, 16)
            sy.dma_start(out=sdp[idx % 2][64:128, :], in_=X.scr_sd[h2:h2 + 1, :].partition_broadcast(64)).then_inc(s2_sd[idx % 2].h, 16)
            sy.dma_start(out=ddp[idx % 2][0:64, :], in_=X.scr_dtd[h1:h1 + 1, :].partition_broadcast(64)).then_inc(s2_sd[idx % 2].h, 16)
            sy.dma_start(out=ddp[idx % 2][64:128, :], in_=X.scr_dtd[h2:h2 + 1, :].partition_broadcast(64)).then_inc(s2_sd[idx % 2].h, 16)
            s2_sd[idx % 2].inc(64)
            if idx >= 1:
                tp_ = pairs[idx - 1]
                sy.wait_ge(X.vs.h, ck(f"yg{idx-1}"))
                sy.dma_start(out=X.scr_yg[tp_ * 128: tp_ * 128 + 128, :],
                             in_=ypair[(idx - 1) % 2][:]).then_inc(s2_spl[(idx - 1) % 2].h, 16)
                s2_spl[(idx - 1) % 2].inc(16)
        sy.wait_ge(X.vs.h, ck(f"yg{NP-1}"))
        sy.dma_start(out=X.scr_yg[pairs[-1] * 128: pairs[-1] * 128 + 128, :],
                     in_=ypair[(NP - 1) % 2][:]).then_inc(s2_spl[(NP - 1) % 2].h, 16)
        s2_spl[(NP - 1) % 2].inc(16)

    @block.tensor
    def _(t):
        t.wait_ge(X.vs.h, ck("dtu"))
        for idx, t_ in enumerate(pairs):
            π = idx % 2
            p_ = idx // 8
            if idx % 8 == 0:
                t.wait_ge(s2_bc.h, 96 * (p_ + 1))
                if idx > 0:
                    t.wait_ge(X.vs.h, ck(f"ya{idx-1}"))
                unit = 0
                for half in range(2):
                    for c in range(NCHUNK):
                        psx = ps_g if unit % 2 == 0 else ps_yd
                        if unit >= 2:
                            t.wait_ge(X.vs.h, ck(f"gm{p_}_{unit-2}"))
                        mi = None
                        for st in range(2):
                            mi = t.matmul(out=psx[:, st * 256:(st + 1) * 256],
                                          lhsT=mbuf[half][0:64, c * CH + st * 128: c * CH + (st + 1) * 128],
                                          rhs=mbuf[half][0:64, S + c * CH: S + (c + 1) * CH],
                                          start=True, stop=True)
                        mi.then_inc(s2_gq.h, 1); s2_gq.inc()
                        unit += 1
            t.wait_ge(s2_wh[π].h, 16 * (idx // 2 + 1))
            if idx >= 1:
                t.wait_ge(X.ss.h, ck(f"gev{idx-1}"))
                t.wait_ge(X.ss.h, ck(f"ev2_{idx-1}"))
            else:
                t.wait_ge(s2_bc.h, 96)
            lhs = lambda k: wbuf[π][:, k * 128:(k + 1) * 128]
            for half, ps in ((0, ps_a), (1, ps_b)):
                for nb in range(2):
                    for k in range(NKT):
                        t.matmul(out=ps[:, nb * 512:(nb + 1) * 512], lhsT=lhs(k),
                                 rhs=xk(k, 3 + half * 1024 + nb * 512, 3 + half * 1024 + (nb + 1) * 512),
                                 start=(k == 0), stop=(k == NKT - 1))
            mi = None
            for k in range(NKT):
                mi = t.matmul(out=ps_h[:, 0:HALO], lhsT=lhs(k), rhs=xk(k, 0, HALO),
                              start=(k == 0), stop=(k == NKT - 1))
            mi.then_inc(s2_mm.h, 1); s2_mm.inc()
            t.wait_ge(X.ss.h, ck(f"ev2_{idx}"))
            t.wait_ge(X.vs.h, ck(f"dg2_{idx}"))
            if idx % 8 == 0:
                t.wait_ge(X.vs.h, ck(f"gm{p_}_15"))
            elif idx >= 1:
                t.wait_ge(X.vs.h, ck(f"ya{idx-1}"))
            if idx >= 1:
                t.wait_ge(X.ss.h, ck(f"cve2_{idx-1}"))
            for half in range(2):
                banks = (ps_yd, ps_g) if half == 0 else (ps_a, ps_b)
                mi = None
                for nbl, psx in ((0, banks[0]), (1, banks[1])):
                    b0 = half * 1024 + nbl * 512
                    for k in range(4):
                        mi = t.matmul(out=psx[:, 0:512], lhsT=X.diag4[:, k * 128:(k + 1) * 128],
                                      rhs=hpre[:, b0 + k: b0 + k + 512],
                                      start=(k == 0), stop=(k == 3))
                mi.then_inc(s2_cvm.h, 1)
                s2_cvm.inc()
            t.wait_ge(s2_wg.h, 16 * (idx + 1))
            t.wait_ge(X.ss.h, ck(f"cve2_{idx}"))
            lhg = lambda k: wgbuf[0][:, k * 128:(k + 1) * 128]
            mi = None
            for half, ps in ((0, ps_a), (1, ps_b)):
                for nb in range(2):
                    for k in range(NKT):
                        mi = t.matmul(out=ps[:, nb * 512:(nb + 1) * 512], lhsT=lhg(k),
                                      rhs=xk(k, 3 + half * 1024 + nb * 512, 3 + half * 1024 + (nb + 1) * 512),
                                      start=(k == 0), stop=(k == NKT - 1))
            mi.then_inc(s2_gmm.h, 1); s2_gmm.inc()
            t.wait_ge(X.vs.h, ck(f"h{idx}"))
            for b in range(2):
                bidx = 2 * idx + b
                if bidx >= 1:
                    pi, pb = (bidx - 1) // 2, (bidx - 1) % 2
                    t.wait_ge(X.ss.h, ck(f"te2_{pi}_{pb}"))
                ti = None
                for cc in range(4):
                    c = b * 4 + cc
                    for st in range(2):
                        ti = t.transpose(out=ps_t[:, (cc * 2 + st) * 128:(cc * 2 + st + 1) * 128],
                                         in_=hpair[:, c * CH + st * 128: c * CH + (st + 1) * 128],
                                         identity=X.identb[:])
                ti.then_inc(s2_tp.h, 1); s2_tp.inc()
            t.wait_ge(X.vs.h, ck(f"m{2 * idx + 1}"))
            t.wait_ge(X.ss.h, ck(f"te2_{idx}_1"))
            if idx >= 1:
                t.wait_ge(X.vs.h, ck(f"ya{idx-1}"))
            for c in range(NCHUNK):
                if c >= 2:
                    t.wait_ge(X.vs.h, ck(f"ya{idx}_{c-2}"))
                psy = ps_yd if c % 2 == 0 else ps_g
                mi = None
                for hh in range(2):
                    for st in range(2):
                        mi = t.matmul(out=psy[hh * 64:(hh + 1) * 64, 0:256],
                                      lhsT=htall[:, (c * 2 + st) * 128 + hh * 64: (c * 2 + st) * 128 + (hh + 1) * 64],
                                      rhs=mbuf[hh][:, (c * 2 + st) * 256:(c * 2 + st + 1) * 256],
                                      start=(st == 0), stop=(st == 1),
                                      tile_position=(0, hh * 64))
                mi.then_inc(s2_yd.h, 1); s2_yd.inc()

    @block.scalar
    def _(sc):
        S_ = X.SC(sc)
        for idx, t_ in enumerate(pairs):
            tc = t_
            S_.wait(s2_mm.h, idx + 1)
            if idx >= 1:
                S_.wait(s2_cvm.h, 2 * idx)
            S_.op("copy", out=hpre[:, 0:HALO], in_=ps_h[:, 0:HALO])
            S_.op("copy", out=hpre[:, HALO:HALO + 1024], in_=ps_a[:, 0:1024])
            S_.op("copy", out=hpre[:, HALO + 1024:SH], in_=ps_b[:, 0:1024])
            S_.ck(f"ev2_{idx}")
            if idx >= 1:
                S_.wait(X.vs.h, ck(f"u{idx-1}"))
                S_.wait(X.vs.h, ck(f"yg{idx-1}"))
            S_.wait(s2_cvm.h, 2 * idx + 1)
            S_.op("copy", out=cpre[:, 0:512], in_=ps_yd[:, 0:512])
            S_.op("copy", out=cpre[:, 512:1024], in_=ps_g[:, 0:512])
            S_.wait(s2_cvm.h, 2 * idx + 2)
            S_.op("copy", out=cpre[:, 1024:1536], in_=ps_a[:, 0:512])
            S_.op("copy", out=cpre[:, 1536:2048], in_=ps_b[:, 0:512])
            S_.ck(f"cve2_{idx}")
            S_.op("activation", out=thb[:], in_=cpre[:], func=AF.Tanh, scale=0.5,
                  bias=X.hconvb_sb[:, tc:tc + 1])
            S_.ck(f"th2_{idx}")
            for hh in range(2):
                hidx = 2 * idx + hh
                if idx >= 1 and hh == 0:
                    S_.wait(s2_yd.h, 8 * idx)
                if hh == 0:
                    S_.wait(s2_gq.h, 16 * (idx // 8 + 1))
                for half in range(2):
                    S_.wait(X.vs.h, ck(f"dh{hidx}_{half}"))
                    S_.op("activation", out=mbuf[hh][:, half * 2048:(half + 1) * 2048],
                          in_=dsc[:], func=AF.Exp)
                    S_.ck(f"e{hidx}_{half}")
            for b in range(2):
                S_.wait(s2_tp.h, 2 * idx + b + 1)
                S_.op("copy", out=htall[:, b * 1024:(b + 1) * 1024], in_=ps_t[:, 0:1024])
                S_.ck(f"te2_{idx}_{b}")
            S_.wait(s2_gmm.h, idx + 1)
            if idx >= 1:
                S_.wait(X.vs.h, ck(f"yg{idx-1}"))
            S_.op("copy", out=gsb[:, 0:1024], in_=ps_a[:, 0:1024])
            S_.op("copy", out=gsb[:, 1024:2048], in_=ps_b[:, 0:1024])
            S_.ck(f"gev{idx}")
            S_.wait(X.vs.h, ck(f"u{idx}"))
            S_.op("activation", out=thb[:, 0:1024], in_=gsb[:, 0:1024], func=AF.Tanh, scale=0.5)
            S_.op("activation", out=thb[:, 1024:2048], in_=gsb[:, 1024:2048], func=AF.Tanh, scale=0.5)
            S_.ck(f"gth{idx}")

    @block.vector
    def _(v):
        V = X.VC(v)
        for idx, t_ in enumerate(pairs):
            π = idx % 2
            p_ = idx // 8
            tc = t_
            if idx % 8 == 0:
                for unit in range(16):
                    psx = ps_g if unit % 2 == 0 else ps_yd
                    half, c = unit // 8, unit % 8
                    V.wait(s2_gq.h, 16 * p_ + unit + 1)
                    V.op("tensor_mul", out=gcache[:, half * 4096 + c * 512: half * 4096 + (c + 1) * 512],
                         in0=psx[:, 0:512], in1=X.mask_sb[:])
                    V.ck(f"gm{p_}_{unit}")
            if idx >= 1:
                V.wait(s2_cvm.h, 2 * idx)
            for k in range(4):
                V.op("tensor_scalar_mul", out=X.diag4[:, k * 128:(k + 1) * 128],
                     in0=X.identb[:], scalar1=X.convw_sb[:, tc * 4 + k:tc * 4 + k + 1])
            V.ck(f"dg2_{idx}")
            for hh in range(2):
                hidx = 2 * idx + hh
                hab = 2 * t_ + hh
                V.wait(s2_ab[hidx % 2].h, 16 * (hidx // 2 + 1))
                for half in range(2):
                    if hidx > 0 or half > 0:
                        pr_h, pr_half = (hidx, half - 1) if half > 0 else (hidx - 1, 1)
                        V.wait(X.ss.h, ck(f"e{pr_h}_{pr_half}"))
                    in0 = acbc[hidx % 2][:, half * 1024:(half + 1) * 1024] \
                        .rearrange("p (c l) -> p c l", l=CH)
                    for st in range(2):
                        in1 = X.biasDT[:].rearrange("p (c st h) -> p c st h", st=2, h=64) \
                            [:, half * 4:(half + 1) * 4, st, hab:hab + 1] \
                            .to_broadcast([128, 4, CH])
                        outD = dsc[:].rearrange("p (c st l) -> p c st l", st=2, l=CH)[:, :, st, :]
                        V.op("scalar_tensor_tensor", out=outD, in0=in0, scalar=1.0, in1=in1,
                             op0=ALU.mult, op1=ALU.add)
                    V.op("tensor_scalar_min", out=dsc[:], in0=dsc[:], scalar1=80.0)
                    V.ck(f"dh{hidx}_{half}")
                    if half == 1:
                        V.ck(f"d{hidx}")
                V.wait(X.ss.h, ck(f"e{hidx}_1"))
                V.op("tensor_mul", out=mbuf[hh][:], in0=mbuf[hh][:],
                     in1=gcache[:, hh * 4096:(hh + 1) * 4096])
                V.ck(f"m{hidx}")
            V.wait(X.ss.h, ck(f"th2_{idx}"))
            if idx >= 1:
                V.wait(s2_tp.h, 2 * idx)
            V.op("tensor_scalar", out=thb[:], in0=thb[:], scalar1=0.5, scalar2=0.5,
                 op0=ALU.mult, op1=ALU.add)
            V.op("scalar_tensor_tensor", out=hpair[:], in0=cpre[:], scalar=X.convb_sb[:, tc:tc + 1],
                 in1=thb[:], op0=ALU.add, op1=ALU.mult)
            V.ck(f"h{idx}")
            V.wait(s2_sd[π].h, 64 * (idx // 2 + 1))
            V.op("tensor_mul", out=thb[:], in0=bsil[:], in1=hpair[:])
            V.op("tensor_mul", out=thb[:], in0=thb[:], in1=ddp[π][:])
            V.op("reduce_sum", out=stcol[:].rearrange("p (c k) -> p c k", k=1),
                 in_=thb[:].rearrange("p (c k) -> p c k", k=CH), axis=AX.X)
            V.op("tensor_tensor", out=cpre[:].rearrange("p (c k) -> p c k", k=CH),
                 in0=csil[:].rearrange("p (c k) -> p c k", k=CH),
                 in1=stcol[:].rearrange("p (c k) -> p c k", k=1).to_broadcast([128, NCHUNK, CH]),
                 op=ALU.mult)
            V.op("tensor_mul", out=cpre[:], in0=cpre[:], in1=sdp[π][:])
            if idx >= 2:
                V.wait(s2_spl[π].h, 16 * (idx // 2))
            V.op("scalar_tensor_tensor", out=ypair[π][:], in0=hpair[:], scalar=X.dcol_sb[:, t_:t_ + 1],
                 in1=cpre[:], op0=ALU.mult, op1=ALU.add)
            V.ck(f"u{idx}")
            for c in range(NCHUNK):
                V.wait(s2_yd.h, 8 * idx + c + 1)
                psy = ps_yd if c % 2 == 0 else ps_g
                V.op("tensor_add", out=ypair[π][:, c * CH:(c + 1) * CH],
                     in0=ypair[π][:, c * CH:(c + 1) * CH], in1=psy[:, 0:256])
                V.ck(f"ya{idx}_{c}")
            V.ck(f"ya{idx}")
            V.wait(X.ss.h, ck(f"gth{idx}"))
            V.op("tensor_scalar", out=thb[:], in0=thb[:], scalar1=0.5, scalar2=0.5,
                 op0=ALU.mult, op1=ALU.add)
            V.op("tensor_mul", out=thb[:], in0=thb[:], in1=gsb[:])
            V.op("tensor_mul", out=ypair[π][:], in0=ypair[π][:], in1=thb[:])
            V.op("tensor_mul", out=dsc[:], in0=ypair[π][:], in1=ypair[π][:])
            V.op("tensor_add", out=X.acc[:], in0=X.acc[:], in1=dsc[:])
            V.ck(f"yg{idx}")

    X.final.append((s2_spl[0], s2_spl[0].n))
    X.final.append((s2_spl[1], s2_spl[1].n))


def _stage3(X):
    block = X.block
    ck = X.ck
    csb, cps = X.csb, X.cps
    acc = X.acc
    NP = X.NP
    s2_spl = X.s2_spl

    wout_sb = csb("wout_sb", [128, 32 * 2048], BF16)
    ygst = [csb(f"ygst{i}", [128, 4096], BF16) for i in range(2)]

    xres = [csb(f"xres{i}", [128, 2048], F32) for i in range(2)]
    o1b = [csb(f"o1b{i}", [128, 2048], BF16) for i in range(2)]
    x2b = [csb(f"x2b{i}", [128, 2048], BF16) for i in range(2)]
    x2st = csb("x2st", [128, 2048], BF16)
    cols3 = csb("cols3", [128, 16], F32)
    wp1 = [csb(f"wp1_{i}", [128, 1920], F32) for i in range(2)]
    rcol = csb("rcol", [128, 16], F32)

    p1all = []
    for r in range(16):
        for c0, nc_ in ((0, 1920), (1920, 1920), (3840, 1792)):
            p1all.append((X.w1_d, r, c0, nc_, r, c0 > 0))
    for r in range(16):
        for c0, nc_ in ((0, 1920), (1920, 1920), (3840, 1792)):
            p1all.append((X.w2_d, r, c0, nc_, 16 + r, c0 > 0))
    for r in range(44):
        p1all.append((X.w3_d, r, 0, 2048, r, False))
    X.p1all = p1all
    NS3 = 96
    X.NS3 = NS3

    ps_o = cps("ps_o", [128, 2048], F32)
    ps_t3 = cps("ps_t3", [128, 1024], BF16)
    ps_v = cps("ps_v", [128, 512], F32)

    s3_w = [X.sem(f"s3_w{q}") for q in range(8)]
    s3_p1 = [X.sem("s3_p10"), X.sem("s3_p11")]
    s3_gt = X.sem("s3_gt"); s3_gb = X.sem("s3_gb"); s3_gw = X.sem("s3_gw")
    s3_yl = [X.sem("s3_yl0"), X.sem("s3_yl1")]
    s3_xl = [X.sem("s3_xl0"), X.sem("s3_xl1")]
    s3_vm = X.sem("s3_vm"); s3_rb = X.sem("s3_rb"); s3_mm = X.sem("s3_mm")
    s3_tp = X.sem("s3_tp")
    s3_os = [X.sem("s3_os0"), X.sem("s3_os1")]
    s3_xs = X.sem("s3_xs")
    X.s3_os = s3_os; X.s3_xs = s3_xs

    NM = 16

    @block.gpsimd
    def _(g):
        g.wait_ge(X.vs.h, ck(f"yg{NP-1}"))
        for q in range(8):
            g.dma_start(out=wout_sb[:, q * 8192:(q + 1) * 8192],
                        in_=X.wout_d[:, q * 8192:(q + 1) * 8192]
                        ).then_inc(s3_w[q].h, 16)
            s3_w[q].inc(16)
        for i in range(NS3):
            wd, r, c0, nc_, pc, accf = p1all[i]
            if i >= 2:
                g.wait_ge(X.vs.h, ck(f"p4_{i-2}"))
            g.dma_start(out=wp1[i % 2][:, 0:nc_], in_=wd[r * 128:(r + 1) * 128, c0:c0 + nc_]
                        ).then_inc(s3_p1[i % 2].h, 16)
            s3_p1[i % 2].inc(16)

    @block.tensor
    def _(t):
        t.wait_ge(X.vs.h, ck(f"yg{NP-1}"))
        for q in range(4):
            if q >= 1:
                t.wait_ge(X.ss.h, ck(f"ve{q-1}"))
            t.matmul(out=ps_v[0:1, 0:512], lhsT=X.ones_sb[:], rhs=acc[:, q * 512:(q + 1) * 512],
                     start=True, stop=True).then_inc(s3_vm.h, 1)
            s3_vm.inc()
        t.wait_ge(X.ss.h, ck("rstd"))
        ti = None
        for m in range(NM):
            ti = t.matmul(out=ps_v[:, m:m + 1], lhsT=acc[0:1, m * 128:(m + 1) * 128],
                          rhs=X.identf[0:1, 0:1], start=True, stop=True)
        ti.then_inc(s3_vm.h, 1); s3_vm.inc()
        def x2trans(m):
            t.wait_ge(X.vs.h, ck(f"x2_{m}"))
            for b in range(2):
                bidx = 2 * m + b
                if bidx >= 1:
                    pi, pb = (bidx - 1) // 2, (bidx - 1) % 2
                    t.wait_ge(X.ss.h, ck(f"te3_{pi}_{pb}"))
                ti = None
                for d in range(8):
                    dt_i = b * 8 + d
                    ti = t.transpose(out=ps_t3[:, d * 128:(d + 1) * 128],
                                     in_=x2b[m % 2][:, dt_i * 128:(dt_i + 1) * 128],
                                     identity=X.identb[:])
                ti.then_inc(s3_tp.h, 1); s3_tp.inc()
        for m in range(NM):
            t.wait_ge(s3_yl[m % 2].h, 16 * (m // 2 + 1))
            t.wait_ge(X.vs.h, ck("wn7"))
            if m >= 1:
                t.wait_ge(X.vs.h, ck(f"o1_{m-1}"))
            mi = None
            for nb in range(4):
                for k in range(32):
                    mi = t.matmul(out=ps_o[:, nb * 512:(nb + 1) * 512],
                                  lhsT=ygst[m % 2][:, k * 128:(k + 1) * 128],
                                  rhs=wout_sb[:, k * 2048 + nb * 512: k * 2048 + (nb + 1) * 512],
                                  start=(k == 0), stop=(k == 31))
            mi.then_inc(s3_mm.h, 1); s3_mm.inc()
            if m >= 1:
                x2trans(m - 1)
        x2trans(NM - 1)
        t.wait_ge(X.vs.h, ck("gv4_0"))
        t.wait_ge(X.ss.h, ck("rce"))
        t.matmul(out=ps_v[0:1, 0:1], lhsT=X.ones_sb[:], rhs=X.parts[:, 48:49],
                 start=True, stop=True).then_inc(s3_gt.h, 1)
        s3_gt.inc()
        t.wait_ge(X.vs.h, ck("gv4_1"))
        t.wait_ge(X.ss.h, ck("ge4_0"))
        t.matmul(out=ps_v[0:1, 0:1], lhsT=X.ones_sb[:], rhs=X.parts[:, 49:50],
                 start=True, stop=True).then_inc(s3_gt.h, 1)
        s3_gt.inc()

    @block.scalar
    def _(sc):
        S_ = X.SC(sc)
        for q in range(4):
            S_.wait(s3_vm.h, q + 1)
            S_.op("copy", out=acc[0:1, q * 512:(q + 1) * 512], in_=ps_v[0:1, 0:512])
            S_.ck(f"ve{q}")
        S_.op("activation", out=acc[0:1, :], in_=acc[0:1, :], func=AF.Ln, scale=1.0 / INTER, bias=X.fconst[0:1, 1:2])
        S_.op("activation", out=acc[0:1, :], in_=acc[0:1, :], func=AF.Exp, scale=-0.5)
        S_.ck("rstd")
        S_.wait(s3_vm.h, 5)
        S_.op("copy", out=rcol[:], in_=ps_v[:, 0:16])
        S_.ck("rce")
        for m in range(NM):
            S_.wait(X.vs.h, ck(f"o1_{m}"))
            if m >= 1:
                S_.wait(X.vs.h, ck(f"st3_{m-1}"))
            S_.op("activation", out=xres[m % 2][:], in_=o1b[m % 2][:], func=AF.Square,
                  accum_out=cols3[:, 1:2])
            S_.ck(f"sq3_{m}")
            S_.wait(X.vs.h, ck(f"st3_{m}"))
            S_.op("activation", out=cols3[:, 6:7], in_=cols3[:, 5:6], func=AF.Ln, bias=X.fconst[:, 0:1])
            S_.op("activation", out=cols3[:, 7:8], in_=cols3[:, 6:7], func=AF.Exp, scale=-0.5)
            S_.ck(f"rs3_{m}")
            for b in range(2):
                S_.wait(s3_tp.h, 2 * m + b + 1)
                if m >= 1 and b == 0:
                    S_.wait(s3_xs.h, 16 * m)
                S_.op("copy", out=x2st[:, b * 1024:(b + 1) * 1024], in_=ps_t3[:, 0:1024])
                S_.ck(f"te3_{m}_{b}")
        S_.wait(s3_gt.h, 1)
        S_.op("copy", out=X.gcol[0:1, 1:2], in_=ps_v[0:1, 0:1])
        S_.ck("ge4_0")
        S_.wait(s3_gt.h, 2)
        S_.op("copy", out=X.gcol[0:1, 2:3], in_=ps_v[0:1, 0:1])
        S_.ck("ge4_1")

    @block.sync
    def _(sy):
        sy.wait_ge(s2_spl[0].h, s2_spl[0].n)
        sy.wait_ge(s2_spl[1].h, s2_spl[1].n)
        def store(m):
            sy.wait_ge(X.ss.h, ck(f"te3_{m}_1"))
            sy.dma_start(out=X.scr_x2t[:, m * 128:(m + 1) * 128].rearrange("(k p) s -> p k s", p=128),
                         in_=x2st[:].rearrange("p (k s) -> p k s", s=128)).then_inc(s3_xs.h, 16)
            s3_xs.inc(16)
            sy.wait_ge(X.vs.h, ck(f"x2_{m}"))
            sy.dma_start(out=X.scr_out1[m * 128:(m + 1) * 128, :], in_=o1b[m % 2][:]).then_inc(s3_os[m % 2].h, 16)
            s3_os[m % 2].inc(16)
        for m in range(NM):
            if m >= 2:
                sy.wait_ge(s3_mm.h, m - 1)
            sy.dma_start(out=ygst[m % 2][:].rearrange("p (k s) -> p k s", k=32),
                         in_=X.scr_yg[:, m * 128:(m + 1) * 128].rearrange("(k p) s -> p k s", p=128)
                         ).then_inc(s3_yl[m % 2].h, 16)
            s3_yl[m % 2].inc(16)
            if m >= 2:
                sy.wait_ge(X.vs.h, ck(f"o1_{m-2}"))
            sy.dma_start(out=xres[m % 2][:], in_=X.x_d[HALO + m * 128: HALO + (m + 1) * 128, :]
                         ).then_inc(s3_xl[m % 2].h, 16)
            s3_xl[m % 2].inc(16)
            if m >= 1:
                store(m - 1)
        store(NM - 1)
        sy.wait_ge(X.ss.h, ck("ge4_0"))
        sy.dma_start(out=X.scr_g[0:1, 0:1], in_=X.gcol[0:1, 1:2]).then_inc(s3_gw.h, 16)
        s3_gw.inc(16)
        sy.wait_ge(s3_gw.h, s3_gw.n)
        sy.dma_start(out=X.gcol[:, 0:1], in_=X.scr_g[0:1, 0:1].partition_broadcast(128)
                     ).then_inc(s3_gb.h, 16)
        s3_gb.inc(16)
        sy.wait_ge(X.ss.h, ck("ge4_1"))
        sy.dma_start(out=X.scr_g[0:1, 1:2], in_=X.gcol[0:1, 2:3]).then_inc(s3_gw.h, 16)
        s3_gw.inc(16)
        sy.wait_ge(s3_gw.h, s3_gw.n)
        sy.wait_ge(X.vs.h, ck("t4_0"))
        sy.dma_start(out=X.gcol[:, 0:1], in_=X.scr_g[0:1, 1:2].partition_broadcast(128)
                     ).then_inc(s3_gb.h, 16)
        s3_gb.inc(16)

    @block.vector
    def _(v):
        V = X.VC(v)
        def chain(m):
            V.wait(s3_mm.h, m + 1)
            V.wait(s3_xl[m % 2].h, 16 * (m // 2 + 1))
            if m >= 2:
                V.wait(s3_os[m % 2].h, 16 * (m // 2))
                V.wait(X.ss.h, ck(f"sq3_{m-1}"))
            V.wait(X.ss.h, ck("rce"))
            V.op("scalar_tensor_tensor", out=o1b[m % 2][:], in0=ps_o[:],
                 scalar=rcol[:, m:m + 1], in1=xres[m % 2][:], op0=ALU.mult, op1=ALU.add)
            V.ck(f"o1_{m}")
            V.op("reduce_sum", out=cols3[:, 0:1], in_=o1b[m % 2][:], axis=AX.X)
            V.wait(X.ss.h, ck(f"sq3_{m}"))
            V.op("tensor_scalar_mul", out=cols3[:, 2:3], in0=cols3[:, 0:1], scalar1=1.0 / E)
            V.op("tensor_scalar_mul", out=cols3[:, 3:4], in0=cols3[:, 1:2], scalar1=1.0 / E)
            V.op("tensor_mul", out=cols3[:, 4:5], in0=cols3[:, 2:3], in1=cols3[:, 2:3])
            V.op("tensor_sub", out=cols3[:, 5:6], in0=cols3[:, 3:4], in1=cols3[:, 4:5])
            V.ck(f"st3_{m}")
            V.wait(X.ss.h, ck(f"rs3_{m}"))
            V.op("scalar_tensor_tensor", out=cols3[:, 8:9], in0=cols3[:, 2:3], scalar=-1.0,
                 in1=cols3[:, 7:8], op0=ALU.mult, op1=ALU.mult)
            if m >= 2:
                V.wait(s3_tp.h, 2 * (m - 1))
            V.op("tensor_scalar", out=x2b[m % 2][:], in0=o1b[m % 2][:],
                 scalar1=cols3[:, 7:8], scalar2=cols3[:, 8:9],
                 op0=ALU.mult, op1=ALU.add)
            V.ck(f"x2_{m}")
        p1i = [0]
        def p1drain(upto):
            while p1i[0] < min(upto, NS3):
                i = p1i[0]
                wd, r, c0, nc_, pc, accf = p1all[i]
                V.wait(s3_p1[i % 2].h, 16 * (i // 2 + 1))
                V.op("reduce_sum", out=X.parts[:, (51 if accf else pc):(52 if accf else pc + 1)],
                     in_=wp1[i % 2][:, 0:nc_], axis=AX.X, apply_absolute_value=True)
                if accf:
                    V.op("tensor_add", out=X.parts[:, pc:pc + 1], in0=X.parts[:, pc:pc + 1],
                         in1=X.parts[:, 51:52])
                V.ck(f"p4_{i}")
                p1i[0] += 1
                if i == 47:
                    V.op("reduce_sum", out=X.parts[:, 48:49], in_=X.parts[:, 0:16], axis=AX.X)
                    V.ck("gv4_0")
                elif i == 95:
                    V.op("reduce_sum", out=X.parts[:, 49:50], in_=X.parts[:, 16:32], axis=AX.X)
                    V.ck("gv4_1")
        for q in range(8):
            V.wait(s3_w[q].h, 16)
            V.op("tensor_tensor", out=wout_sb[:, q * 8192:(q + 1) * 8192].rearrange("p (k e) -> p k e", k=4),
                 in0=wout_sb[:, q * 8192:(q + 1) * 8192].rearrange("p (k e) -> p k e", k=4),
                 in1=X.normw_sb[:, q * 4:(q + 1) * 4].rearrange("p (k o) -> p k o", o=1)
                 .to_broadcast([128, 4, 2048]),
                 op=ALU.mult)
            V.ck(f"wn{q}")
        for m in range(NM):
            if m >= 1:
                chain(m - 1)
            p1drain((m + 1) * 6)
        chain(NM - 1)
        p1drain(NS3)
        V.wait(s3_gb.h, 16)
        V.op("tensor_scalar", out=X.tpos[:, 0:1], in0=X.gcol[:, 0:1], scalar1=0.5 / NW,
             scalar2=0.5e-8, op0=ALU.mult, op1=ALU.add)
        V.op("tensor_scalar_mul", out=X.tpos[:, 1:2], in0=X.tpos[:, 0:1], scalar1=-1.0)
        V.ck("t4_0")
        V.wait(s3_gb.h, 32)
        V.op("tensor_scalar", out=X.tpos[:, 2:3], in0=X.gcol[:, 0:1], scalar1=0.5 / NW,
             scalar2=0.5e-8, op0=ALU.mult, op1=ALU.add)
        V.op("tensor_scalar_mul", out=X.tpos[:, 3:4], in0=X.tpos[:, 2:3], scalar1=-1.0)
        V.ck("t4_1")

    X.final.append((s3_os[0], s3_os[0].n))
    X.final.append((s3_os[1], s3_os[1].n))
    X.final.append((s3_xs, 16 * NM))


def _stage4(X):
    block = X.block
    ck = X.ck
    nc = X.nc
    p1all = X.p1all
    w3p1 = p1all[96:140]
    MS = 20
    NG = 11

    with ExitStack() as f4b:
        fsb = lambda name, shp, dt: f4b.enter_context(nc.sbuf_tensor(name, shp, dt))
        fps = lambda name, shp, dt: f4b.enter_context(nc.psum_tensor(name, shp, dt))
        x2T = fsb("x2T", [128, NKT * 2048], BF16)
        qgrp = [[fsb(f"qgrp{mi}_{i}", [128, NKT * 512], BF16) for i in range(2)]
                for mi in range(2)]
        sgb = fsb("sgb", [128, 2048], BF16)
        hidb = [fsb(f"hidb{i}", [128, 2048], BF16) for i in range(2)]
        wq2 = [fsb(f"wq2_{i}", [128, 512], F32) for i in range(4)]
        qt2 = fsb("qt2", [128, 512], BF16)
        wq3 = [fsb(f"wq3_{i}", [128, 2048], F32) for i in range(2)]
        qb3 = [fsb(f"qb3_{i}", [128, 2048], BF16) for i in range(2)]
        qt3 = fsb("qt3", [128, 2048], BF16)
        grow = fsb("grow", [1, 128], F32)
        ps_x1 = fps("ps_x1", [128, 2048], F32)
        ps_x2 = fps("ps_x2", [128, 2048], F32)

        s4_x2 = X.sem("s4_x2")
        s4_wl = [X.sem(f"s4_wl{i}") for i in range(4)]
        s4_mmb = X.sem("s4_mmb")
        s4_hs = [X.sem("s4_hs0"), X.sem("s4_hs1")]
        s4_p3l = [X.sem("s4_p3l0"), X.sem("s4_p3l1")]
        s4_w3l = [X.sem("s4_w3l0"), X.sem("s4_w3l1")]
        s4_q3s = [X.sem("s4_q3s0"), X.sem("s4_q3s1")]
        s4_gw = X.sem("s4_gw"); s4_gr = X.sem("s4_gr"); s4_gb = X.sem("s4_gb")
        X.s4_hs = s4_hs
        X.s4_q3s = s4_q3s
        s3_os, s3_xs = X.s3_os, X.s3_xs

        def w3blocks(m):
            if m < MS:
                return []
            return [b for b in (2 * (m - MS), 2 * (m - MS) + 1) if b < NFT]

        @block.sync
        def _(sy):
            sy.wait_ge(s3_os[0].h, s3_os[0].n)
            sy.wait_ge(s3_os[1].h, s3_os[1].n)
            sy.wait_ge(s3_xs.h, 256)
            sy.wait_ge(X.vs.h, ck("t4_1"))
            sy.wait_ge(X.ss.h, ck("ge4_1"))
            for q in range(8):
                sy.dma_start(out=x2T[:, q * 4096:(q + 1) * 4096].rearrange("p (k s) -> p k s", k=2),
                             in_=X.scr_x2t[q * 256:(q + 1) * 256, :].rearrange("(k p) s -> p k s", p=128)
                             ).then_inc(s4_x2.h, 16)
                s4_x2.inc(16)
            p3q = [0]
            def p3issue(upto):
                while p3q[0] < min(upto, NFT):
                    i = p3q[0]
                    wd, r, c0, nc_, pc, accf = w3p1[i]
                    if i >= 2:
                        sy.wait_ge(X.ss.h, ck(f"p4w3_{i-2}"))
                    sy.dma_start(out=wq3[i % 2][:], in_=wd[r * 128:(r + 1) * 128, :]
                                 ).then_inc(s4_p3l[i % 2].h, 16)
                    s4_p3l[i % 2].inc(16)
                    p3q[0] += 1
            for c in range(NG + 1):
                for j in (range(32) if c < NG else []):
                    ci = c * 32 + j
                    mi_, r = j // 16, j % 16
                    wd = X.w1_d if mi_ == 0 else X.w2_d
                    if ci >= 4:
                        sy.wait_ge(X.vs.h, ck(f"qg_{ci-4}"))
                    sy.dma_start(out=wq2[ci % 4][:], in_=wd[r * 128:(r + 1) * 128, c * 512:(c + 1) * 512]
                                 ).then_inc(s4_wl[ci % 4].h, 16)
                    s4_wl[ci % 4].inc(16)
                if c == 0:
                    continue
                for m in range(4 * (c - 1), min(4 * (c - 1) + 4, NFT)):
                    p3issue((m + 1) * 3)
                    if m == 16:
                        sy.wait_ge(X.vs.h, ck("gvp"))
                        sy.dma_start(out=X.scr_gt[0:1, :].rearrange("o p -> p o", p=128),
                                     in_=X.parts[:, 50:51]).then_inc(s4_gw.h, 16)
                        s4_gw.inc(16)
                        sy.wait_ge(s4_gw.h, s4_gw.n)
                        sy.dma_start(out=grow[0:1, :], in_=X.scr_gt[0:1, :]).then_inc(s4_gr.h, 16)
                        s4_gr.inc(16)
                        sy.wait_ge(X.vs.h, ck("gvw3"))
                        sy.dma_start(out=X.scr_g[0:1, 2:3], in_=X.gcol[0:1, 3:4]).then_inc(s4_gw.h, 16)
                        s4_gw.inc(16)
                        sy.wait_ge(s4_gw.h, s4_gw.n)
                        sy.dma_start(out=X.gcol[:, 0:1], in_=X.scr_g[0:1, 2:3].partition_broadcast(128)
                                     ).then_inc(s4_gb.h, 16)
                        s4_gb.inc(16)
                    for b in w3blocks(m):
                        if b < 2:
                            sy.wait_ge(X.ss.h, ck("p4w3_43"))
                        else:
                            sy.wait_ge(X.vs.h, ck(f"q3_{b-2}"))
                        sy.dma_start(out=wq3[b % 2][:], in_=X.w3_d[b * 128:(b + 1) * 128, :]
                                     ).then_inc(s4_w3l[b % 2].h, 16)
                        s4_w3l[b % 2].inc(16)
                        if b >= 1:
                            sy.wait_ge(X.vs.h, ck(f"q3_{b-1}"))
                            sy.dma_start(out=X.scr_q3[(b - 1) * 128:b * 128, :], in_=qb3[(b - 1) % 2][:]
                                         ).then_inc(s4_q3s[(b - 1) % 2].h, 16)
                            s4_q3s[(b - 1) % 2].inc(16)
                    if m >= 1:
                        sy.wait_ge(X.vs.h, ck(f"hd{m-1}"))
                        sy.dma_start(out=X.scr_hid[(m - 1) * 128: m * 128, :], in_=hidb[(m - 1) % 2][:]
                                     ).then_inc(s4_hs[(m - 1) % 2].h, 16)
                        s4_hs[(m - 1) % 2].inc(16)
            sy.wait_ge(X.vs.h, ck(f"q3_{NFT-1}"))
            sy.dma_start(out=X.scr_q3[(NFT - 1) * 128:NFT * 128, :], in_=qb3[(NFT - 1) % 2][:]
                         ).then_inc(s4_q3s[(NFT - 1) % 2].h, 16)
            s4_q3s[(NFT - 1) % 2].inc(16)
            sy.wait_ge(X.vs.h, ck(f"hd{NFT-1}"))
            sy.dma_start(out=X.scr_hid[(NFT - 1) * 128: NFT * 128, :], in_=hidb[(NFT - 1) % 2][:]
                         ).then_inc(s4_hs[(NFT - 1) % 2].h, 16)
            s4_hs[(NFT - 1) % 2].inc(16)

        @block.vector
        def _(v):
            V = X.VC(v)
            for c in range(NG + 1):
                for j in (range(32) if c < NG else []):
                    ci = c * 32 + j
                    mi_, r = j // 16, j % 16
                    V.wait(s4_wl[ci % 4].h, 16 * (ci // 4 + 1))
                    if c >= 2 and j == 0:
                        V.wait(s4_mmb.h, 2 * (4 * (c - 2) + 4))
                    dst = qgrp[mi_][c % 2][:, r * 512:(r + 1) * 512]
                    tp = X.tpos[:, 2 * mi_:2 * mi_ + 1]
                    tn = X.tpos[:, 2 * mi_ + 1:2 * mi_ + 2]
                    V.op("tensor_single_scalar", out=dst, in_=wq2[ci % 4][:], scalar=tp, op=ALU.is_gt)
                    V.op("tensor_single_scalar", out=qt2[:], in_=wq2[ci % 4][:], scalar=tn, op=ALU.is_lt)
                    V.op("tensor_sub", out=dst, in0=dst, in1=qt2[:])
                    V.ck(f"qg_{ci}")
                if c == 0:
                    continue
                for m in range(4 * (c - 1), min(4 * (c - 1) + 4, NFT)):
                    if m == 15:
                        V.wait(X.ss.h, ck("p4w3_43"))
                        V.op("reduce_sum", out=X.parts[:, 50:51], in_=X.parts[:, 0:44], axis=AX.X)
                        V.ck("gvp")
                    if m == 16:
                        V.wait(s4_gr.h, 16)
                        V.op("reduce_sum", out=X.gcol[0:1, 3:4], in_=grow[0:1, :], axis=AX.X)
                        V.ck("gvw3")
                    if m == MS:
                        V.wait(s4_gb.h, 16)
                        V.op("tensor_scalar", out=X.tpos[:, 4:5], in0=X.gcol[:, 0:1], scalar1=0.5 / NW,
                             scalar2=0.5e-8, op0=ALU.mult, op1=ALU.add)
                        V.op("tensor_scalar_mul", out=X.tpos[:, 5:6], in0=X.tpos[:, 4:5], scalar1=-1.0)
                        V.ck("t4_2")
                    V.wait(X.ss.h, ck(f"sg{m}"))
                    V.wait(s4_mmb.h, 2 * m + 2)
                    if m >= 2:
                        V.wait(s4_hs[m % 2].h, 16 * (m // 2))
                    V.op("tensor_mul", out=hidb[m % 2][:], in0=sgb[:], in1=ps_x2[:])
                    V.ck(f"hd{m}")
                    for b in w3blocks(m):
                        V.wait(s4_w3l[b % 2].h, 16 * (b // 2 + 1))
                        if b >= 2:
                            V.wait(s4_q3s[b % 2].h, 16 * ((b - 2) // 2 + 1))
                        V.op("tensor_single_scalar", out=qb3[b % 2][:], in_=wq3[b % 2][:],
                             scalar=X.tpos[:, 4:5], op=ALU.is_gt)
                        V.op("tensor_single_scalar", out=qt3[:], in_=wq3[b % 2][:],
                             scalar=X.tpos[:, 5:6], op=ALU.is_lt)
                        V.op("tensor_sub", out=qb3[b % 2][:], in0=qb3[b % 2][:], in1=qt3[:])
                        V.ck(f"q3_{b}")

        @block.tensor
        def _(t):
            t.wait_ge(s4_x2.h, 128)
            for m in range(NFT):
                c = m // 4
                t.wait_ge(X.vs.h, ck(f"qg_{c * 32 + 31}"))
                if m >= 1:
                    t.wait_ge(X.ss.h, ck(f"sg{m-1}"))
                lof = (m % 4) * 128
                mi = None
                for nb in range(4):
                    for k in range(NKT):
                        mi = t.matmul(out=ps_x1[:, nb * 512:(nb + 1) * 512],
                                      lhsT=qgrp[0][c % 2][:, k * 512 + lof: k * 512 + lof + 128],
                                      rhs=x2T[:, k * 2048 + nb * 512: k * 2048 + (nb + 1) * 512],
                                      start=(k == 0), stop=(k == NKT - 1))
                mi.then_inc(s4_mmb.h, 1)
                if m >= 1:
                    t.wait_ge(X.vs.h, ck(f"hd{m-1}"))
                mi = None
                for nb in range(4):
                    for k in range(NKT):
                        mi = t.matmul(out=ps_x2[:, nb * 512:(nb + 1) * 512],
                                      lhsT=qgrp[1][c % 2][:, k * 512 + lof: k * 512 + lof + 128],
                                      rhs=x2T[:, k * 2048 + nb * 512: k * 2048 + (nb + 1) * 512],
                                      start=(k == 0), stop=(k == NKT - 1))
                mi.then_inc(s4_mmb.h, 1); s4_mmb.inc(2)

        @block.scalar
        def _(sc):
            S_ = X.SC(sc)
            p3i = [0]
            def p3drain(upto):
                while p3i[0] < min(upto, NFT):
                    i = p3i[0]
                    wd, r, c0, nc_, pc, accf = w3p1[i]
                    S_.wait(s4_p3l[i % 2].h, 16 * (i // 2 + 1))
                    S_.op("activation", out=wq3[i % 2][:], in_=wq3[i % 2][:], func=AF.Abs,
                          accum_out=X.parts[:, pc:pc + 1])
                    S_.ck(f"p4w3_{i}")
                    p3i[0] += 1
            for m in range(NFT):
                S_.wait(s4_mmb.h, 2 * m + 1)
                if m >= 1:
                    S_.wait(X.vs.h, ck(f"hd{m-1}"))
                S_.op("activation", out=sgb[:], in_=ps_x1[:], func=AF.Sigmoid)
                S_.ck(f"sg{m}")
                p3drain((m + 1) * 3)

    with ExitStack() as f4c:
        fsb = lambda name, shp, dt: f4c.enter_context(nc.sbuf_tensor(name, shp, dt))
        fps = lambda name, shp, dt: f4c.enter_context(nc.psum_tensor(name, shp, dt))
        q3h = fsb("q3h", [128, NFT * 1024], BF16)
        hist = [fsb(f"hist{i}", [128, NFT * 128], BF16) for i in range(2)]
        o1t = [fsb(f"o1t{i}", [128, 1024], BF16) for i in range(2)]
        outf = [fsb(f"outf{i}", [128, 1024], F32) for i in range(2)]
        ps_o3 = [fps(f"ps_o3{i}", [128, 1024], F32) for i in range(2)]

        s4_q3 = X.sem("s4_q3")
        s4_hl = [X.sem("s4_hl0"), X.sem("s4_hl1")]
        s4_o1l = [X.sem("s4_o1l0"), X.sem("s4_o1l1")]
        s4_mmc = X.sem("s4_mmc")
        s_out = [X.sem("s_out0"), X.sem("s_out1")]
        s4_hs = X.s4_hs

        @block.sync
        def _(sy):
            sy.wait_ge(s4_hs[0].h, s4_hs[0].n)
            sy.wait_ge(s4_hs[1].h, s4_hs[1].n)
            sy.wait_ge(X.s4_q3s[0].h, X.s4_q3s[0].n)
            sy.wait_ge(X.s4_q3s[1].h, X.s4_q3s[1].n)
            for eh in range(2):
                if eh == 1:
                    sy.wait_ge(s4_mmc.h, 16)
                for q in range(4):
                    sy.dma_start(out=q3h[:, q * 11 * 1024:(q + 1) * 11 * 1024].rearrange("p (k e) -> p k e", k=11),
                                 in_=X.scr_q3[q * 1408:(q + 1) * 1408, eh * 1024:(eh + 1) * 1024]
                                 .rearrange("(k p) e -> p k e", p=128)).then_inc(s4_q3.h, 16)
                    s4_q3.inc(16)
                for m in range(16):
                    gi = eh * 16 + m
                    if gi >= 2:
                        sy.wait_ge(s4_mmc.h, gi - 1)
                    sy.dma_start(out=hist[gi % 2][:].rearrange("p (k s) -> p k s", k=NFT),
                                 in_=X.scr_hid[:, m * 128:(m + 1) * 128].rearrange("(k p) s -> p k s", p=128)
                                 ).then_inc(s4_hl[gi % 2].h, 16)
                    s4_hl[gi % 2].inc(16)
                    if gi >= 2:
                        sy.wait_ge(X.vs.h, ck(f"of{gi-2}"))
                    sy.dma_start(out=o1t[gi % 2][:],
                                 in_=X.scr_out1[m * 128:(m + 1) * 128, eh * 1024:(eh + 1) * 1024]
                                 ).then_inc(s4_o1l[gi % 2].h, 16)
                    s4_o1l[gi % 2].inc(16)

        @block.tensor
        def _(t):
            for eh in range(2):
                t.wait_ge(s4_q3.h, 64 * (eh + 1))
                for m in range(16):
                    gi = eh * 16 + m
                    t.wait_ge(s4_hl[gi % 2].h, 16 * (gi // 2 + 1))
                    if gi >= 2:
                        t.wait_ge(X.vs.h, ck(f"of{gi-2}"))
                    mi = None
                    for nb in range(2):
                        for k in range(NFT):
                            mi = t.matmul(out=ps_o3[gi % 2][:, nb * 512:(nb + 1) * 512],
                                          lhsT=hist[gi % 2][:, k * 128:(k + 1) * 128],
                                          rhs=q3h[:, k * 1024 + nb * 512: k * 1024 + (nb + 1) * 512],
                                          start=(k == 0), stop=(k == NFT - 1))
                    mi.then_inc(s4_mmc.h, 1); s4_mmc.inc()

        @block.vector
        def _(v):
            V = X.VC(v)
            for eh in range(2):
                for m in range(16):
                    gi = eh * 16 + m
                    V.wait(s4_mmc.h, gi + 1)
                    V.wait(s4_o1l[gi % 2].h, 16 * (gi // 2 + 1))
                    if gi >= 2:
                        V.wait(s_out[gi % 2].h, 16 * (gi // 2))
                    V.op("tensor_add", out=outf[gi % 2][:], in0=ps_o3[gi % 2][:], in1=o1t[gi % 2][:])
                    V.ck(f"of{gi}")

        @block.gpsimd
        def _(g):
            for eh in range(2):
                for m in range(16):
                    gi = eh * 16 + m
                    g.wait_ge(X.vs.h, ck(f"of{gi}"))
                    g.dma_start(out=X.y_d[m * 128:(m + 1) * 128, eh * 1024:(eh + 1) * 1024],
                                in_=outf[gi % 2][:]).then_inc(s_out[gi % 2].h, 16)
                    s_out[gi % 2].inc(16)
            g.wait_ge(s_out[0].h, s_out[0].n)
            g.wait_ge(s_out[1].h, s_out[1].n)

        X.out_written = True
        X.final = [(s_out[0], s_out[0].n), (s_out[1], s_out[1].n)]


def _finish(X):
    block = X.block
    nc = X.nc
    if X.out_written:
        return
    with nc.sbuf_tensor("zfin", [128, E], F32) as zf, \
         nc.semaphore("finv") as finv, nc.semaphore("fin") as fin:
        @block.vector
        def _(v):
            v.wait_ge(X.vs.h, X.vs.n)
            v.memset(zf[:], 0.0).then_inc(finv, 1)

        @block.gpsimd
        def _(g):
            g.wait_ge(finv, 1)
            for ent in X.final:
                if ent[0] == "ck":
                    g.wait_ge(ent[1].h, X.ck(ent[2]))
                else:
                    g.wait_ge(ent[0].h, ent[1])
            for i in range(S // 128):
                g.dma_start(out=X.y_d[i * 128:(i + 1) * 128, :], in_=zf[:]).then_inc(fin, 16)
            g.wait_ge(fin, 16 * (S // 128))



def _host_inputs(inputs):
    x = np.asarray(inputs["L0_attn_input_states"], dtype=np.float32).reshape(16384, 2048)
    win0 = np.asarray(inputs["L0_attn_W_in"], dtype=np.float32)
    winp = np.concatenate([win0, np.zeros((E, 73 * 128 - PROJ), np.float32)], axis=1)
    win = np.ascontiguousarray(
        winp.reshape(NKT, 128, 73, 128).transpose(2, 1, 0, 3).reshape(73, 128, NKT * 128))
    convw = np.ascontiguousarray(
        np.asarray(inputs["L0_attn_conv_w"], dtype=np.float32).reshape(40, 128, 4)
        .transpose(1, 0, 2).reshape(128, 160))
    convb = np.ascontiguousarray(
        np.asarray(inputs["L0_attn_conv_b"], dtype=np.float32).reshape(40, 128).T)
    dtb = np.ascontiguousarray(np.asarray(inputs["L0_attn_dt_bias"], dtype=np.float32).reshape(NH, 1))
    alog = np.ascontiguousarray(np.asarray(inputs["L0_attn_A_log"], dtype=np.float32).reshape(NH, 1))
    drep = np.ascontiguousarray(
        np.repeat(np.asarray(inputs["L0_attn_D"], dtype=np.float32), 64).reshape(32, 128).T)
    normw = np.ascontiguousarray(
        np.asarray(inputs["L0_attn_norm_w"], dtype=np.float32).reshape(32, 128).T)
    wout0 = np.asarray(inputs["L0_attn_W_out"], dtype=np.float32)
    wout = np.ascontiguousarray(
        wout0.reshape(32, 128, E).transpose(1, 0, 2).reshape(128, 32 * E))
    w1 = np.ascontiguousarray(np.asarray(inputs["L0_ffn_w1"], dtype=np.float32).T)
    w2 = np.ascontiguousarray(np.asarray(inputs["L0_ffn_w2"], dtype=np.float32).T)
    w3 = np.ascontiguousarray(np.asarray(inputs["L0_ffn_w3"], dtype=np.float32).T)
    identb = np.eye(128, dtype=ml_dtypes.bfloat16)
    identf = np.eye(128, dtype=np.float32)
    i0 = np.arange(128)[:, None]
    l0 = np.arange(256)[None, :]
    mask = np.concatenate([(l0 >= i0).astype(np.float32),
                           (l0 >= i0 + 128).astype(np.float32)], axis=0)
    in_maps = []
    for c in range(8):
        r0 = c * S
        if c == 0:
            xc = np.concatenate([np.zeros((HALO, E), np.float32), x[0:S, :]], axis=0)
        else:
            xc = x[r0 - HALO:r0 + S, :]
        in_maps.append(dict(
            x=np.ascontiguousarray(xc), w_in=win, conv_w=convw, conv_b=convb,
            dt_bias=dtb, a_log=alog, d_rep=drep, norm_w=normw, w_out=wout,
            w1=w1, w2=w2, w3=w3, identb=identb, identf=identf, mask=mask,
        ))
    return in_maps


def kernel(**inputs):
    in_maps = _host_inputs(inputs)
    nc = build_nc()
    res = run_bass_kernel_spmd(nc, in_maps, core_ids=list(range(8)))
    out = np.concatenate([np.asarray(r["y"]) for r in res.results], axis=0)
    return out.reshape(1, 16384, 2048).astype(np.float32)


# version 22 (final)
# speedup vs baseline: 1.0104x; 1.0104x over previous
import numpy as np
import ml_dtypes
from contextlib import ExitStack

import concourse.bass as bass
import concourse.mybir as mybir
from concourse.bass_utils import run_bass_kernel_spmd

F32 = mybir.dt.float32
BF16 = mybir.dt.bfloat16
AF = mybir.ActivationFunctionType
ALU = mybir.AluOpType
AX = mybir.AxisListType

S = 2048
HALO = 3
SH = S + HALO
E = 2048
NKT = 16
PROJ = 9280
INTER = 4096
CDIM = 5120
FFH = 5632
NFT = 44
NH = 64
CH = 256
NCHUNK = 8
LN_EPS = 1e-5
RMS_EPS = 1e-6
NW = FFH * E

DBG = False
LAST_STAGE = 9


class Sem:
    def __init__(self, stk, nc, name):
        self.h = stk.enter_context(nc.semaphore(name))
        self.n = 0

    def inc(self, k=1):
        self.n += k
        return self.n


class Chain:

    def __init__(self, eng, sem, rec):
        self.e = eng
        self.s = sem
        self.rec = rec

    def wait(self, h, v):
        if v > 0:
            self.e.wait_ge(h, v)

    def op(self, name, *a, **kw):
        self.e.wait_ge(self.s.h, self.s.n)
        getattr(self.e, name)(*a, **kw).then_inc(self.s.h, 1)
        return self.s.inc()

    def ck(self, key):
        self.rec[key] = self.s.n


class Ctx:
    pass


def build_nc(last_stage=None):
    if last_stage is None:
        last_stage = LAST_STAGE
    rec = {}
    _build(last_stage, {}, rec)
    rec2 = {}
    nc = _build(last_stage, rec, rec2)
    return nc


def _build(last_stage, prev, rec):
    nc = bass.Bass("TRN2", target_bir_lowering=False)
    scrkind = "ExternalOutput" if DBG else "Internal"
    X = Ctx()
    X.nc = nc
    X.last_stage = last_stage
    X.out_written = False
    X.final = []
    X.rec = rec
    X.prev = prev
    X.ck = lambda key: prev.get(key, 0)

    X.x_d = nc.dram_tensor("x", [SH, E], F32, kind="ExternalInput").ap()
    X.win_d = nc.dram_tensor("w_in", [73, 128, NKT * 128], F32, kind="ExternalInput").ap()
    X.convw_d = nc.dram_tensor("conv_w", [128, 160], F32, kind="ExternalInput").ap()
    X.convb_d = nc.dram_tensor("conv_b", [128, 40], F32, kind="ExternalInput").ap()
    X.dtb_d = nc.dram_tensor("dt_bias", [NH, 1], F32, kind="ExternalInput").ap()
    X.alog_d = nc.dram_tensor("a_log", [NH, 1], F32, kind="ExternalInput").ap()
    X.drep_d = nc.dram_tensor("d_rep", [128, 32], F32, kind="ExternalInput").ap()
    X.normw_d = nc.dram_tensor("norm_w", [128, 32], F32, kind="ExternalInput").ap()
    X.wout_d = nc.dram_tensor("w_out", [128, 32 * E], F32, kind="ExternalInput").ap()
    X.w1_d = nc.dram_tensor("w1", [E, FFH], F32, kind="ExternalInput").ap()
    X.w2_d = nc.dram_tensor("w2", [E, FFH], F32, kind="ExternalInput").ap()
    X.w3_d = nc.dram_tensor("w3", [FFH, E], F32, kind="ExternalInput").ap()
    X.identb_d = nc.dram_tensor("identb", [128, 128], BF16, kind="ExternalInput").ap()
    X.identf_d = nc.dram_tensor("identf", [128, 128], F32, kind="ExternalInput").ap()
    X.mask_d = nc.dram_tensor("mask", [256, 256], F32, kind="ExternalInput").ap()
    X.y_d = nc.dram_tensor("y", [S, E], F32, kind="ExternalOutput").ap()

    X.scr_acum = nc.dram_tensor("scr_acum", [NH, S], F32, kind=scrkind).ap()
    X.scr_sd = nc.dram_tensor("scr_sd", [NH, S], BF16, kind=scrkind).ap()
    X.scr_dtd = nc.dram_tensor("scr_dtd", [NH, S], BF16, kind=scrkind).ap()
    X.scr_b = nc.dram_tensor("scr_b", [512, S], BF16, kind=scrkind).ap()
    X.scr_c = nc.dram_tensor("scr_c", [512, S], BF16, kind=scrkind).ap()
    X.scr_yg = nc.dram_tensor("scr_yg", [INTER, S], BF16, kind=scrkind).ap()
    X.scr_out1 = nc.dram_tensor("scr_out1", [S, E], BF16, kind=scrkind).ap()
    X.scr_x2t = nc.dram_tensor("scr_x2t", [E, S], BF16, kind=scrkind).ap()
    X.scr_q3 = nc.dram_tensor("scr_q3", [FFH, E], BF16, kind=scrkind).ap()
    X.scr_hid = nc.dram_tensor("scr_hid", [FFH, S], BF16, kind=scrkind).ap()
    X.scr_row = nc.dram_tensor("scr_row", [1, E], F32, kind=scrkind).ap()
    X.scr_g = nc.dram_tensor("scr_g", [1, 8], F32, kind=scrkind).ap()
    X.scr_gt = nc.dram_tensor("scr_gt", [1, 128], F32, kind=scrkind).ap()

    with ExitStack() as glob, nc.Block() as block:
        X.block = block
        X.glob = glob
        X.sem = lambda name: Sem(glob, nc, name)
        gsb = lambda name, shp, dt: glob.enter_context(nc.sbuf_tensor(name, shp, dt))

        X.vs = X.sem("vself")
        X.ss = X.sem("sself")
        X.VC = lambda v: Chain(v, X.vs, rec)
        X.SC = lambda sc: Chain(sc, X.ss, rec)

        X.identb = gsb("identb_s", [128, 128], BF16)
        X.identf = gsb("identf_s", [128, 128], F32)
        X.mask_sb = gsb("mask_sb", [128, 512], F32)
        X.convw_sb = gsb("convw_sb", [128, 160], F32)
        X.convb_sb = gsb("convb_sb", [128, 40], F32)
        X.hconvb_sb = gsb("hconvb_sb", [128, 40], F32)
        X.dtb_sb = gsb("dtb_sb", [NH, 1], F32)
        X.alog_sb = gsb("alog_sb", [NH, 1], F32)
        X.dcol_sb = gsb("dcol_sb", [128, 32], F32)
        X.normw_sb = gsb("normw_sb", [128, 32], F32)
        X.ones_sb = gsb("ones_sb", [128, 1], F32)
        X.fconst = gsb("fconst", [128, 2], F32)
        X.parts = gsb("parts", [128, 52], F32)
        X.gcol = gsb("gcol", [128, 4], F32)
        X.tpos = gsb("tpos", [128, 8], F32)
        X.acc = gsb("acc", [128, S], F32)

        X.s_c = X.sem("s_c")

        @block.sync
        def _(sy):
            for dst, src in [
                (X.identb[:], X.identb_d), (X.identf[:], X.identf_d),
                (X.mask_sb[:, 0:256], X.mask_d[0:128, :]),
                (X.mask_sb[:, 256:512], X.mask_d[128:256, :]),
                (X.convw_sb[:], X.convw_d), (X.convb_sb[:], X.convb_d),
                (X.dtb_sb[:], X.dtb_d), (X.alog_sb[:], X.alog_d),
                (X.dcol_sb[:], X.drep_d), (X.normw_sb[:], X.normw_d),
            ]:
                sy.dma_start(out=dst, in_=src).then_inc(X.s_c.h, 16)
                X.s_c.inc(16)

        @block.vector
        def _(v):
            V = X.VC(v)
            V.wait(X.s_c.h, X.s_c.n)
            V.op("tensor_scalar_mul", out=X.hconvb_sb[:], in0=X.convb_sb[:], scalar1=0.5)
            V.op("memset", X.acc[:], 0.0)
            V.op("memset", X.ones_sb[:], 1.0)
            V.op("memset", X.fconst[:, 0:1], LN_EPS)
            V.op("memset", X.fconst[:, 1:2], RMS_EPS)
            V.ck("misc")

        with ExitStack() as att:
            X.asb = lambda name, shp, dt: att.enter_context(nc.sbuf_tensor(name, shp, dt))
            X.aps = lambda name, shp, dt: att.enter_context(nc.psum_tensor(name, shp, dt))
            _attn_base(X)
            with ExitStack() as s01:
                X.bsb = lambda name, shp, dt: s01.enter_context(nc.sbuf_tensor(name, shp, dt))
                _stage0(X)
                if last_stage >= 1:
                    _stage1(X)
            if last_stage >= 2:
                _stage2(X)

        if last_stage >= 3:
            with ExitStack() as st3:
                X.csb = lambda name, shp, dt: st3.enter_context(nc.sbuf_tensor(name, shp, dt))
                X.cps = lambda name, shp, dt: st3.enter_context(nc.psum_tensor(name, shp, dt))
                _stage3(X)
        if last_stage >= 4:
            _stage4(X)

        _finish(X)

    return nc


def _attn_base(X):
    asb, aps = X.asb, X.aps
    X.xlnT = asb("xlnT", [128, NKT * SH], BF16)
    X.dsc = asb("dsc", [128, 2048], F32)
    X.wbuf = [asb(f"wbuf{i}", [128, NKT * 128], BF16) for i in range(2)]
    X.wgbuf = [asb("wgbuf0", [128, NKT * 128], BF16)]
    X.hpre = asb("hpre", [128, SH], BF16)
    X.cpre = asb("cpre", [128, S], BF16)
    X.thb = asb("thb", [128, S], BF16)
    X.biasDT = asb("biasDT", [128, 1024], F32)
    X.diag4 = asb("diag4", [128, 512], BF16)
    X.cols = asb("cols", [128, 16], F32)

    X.ps_a = aps("ps_a", [128, 1024], F32)
    X.ps_b = aps("ps_b", [128, 1024], F32)
    X.ps_h = aps("ps_h", [128, 512], F32)
    X.ps_t = aps("ps_t", [128, 1024], BF16)
    X.ps_yd = aps("ps_yd", [128, 512], F32)
    X.ps_g = aps("ps_g", [128, 512], F32)

    X.xk = lambda k, a, b: X.xlnT[:, k * SH + a: k * SH + b]


def _stage0(X):
    block = X.block
    ck = X.ck
    xs = [X.bsb(f"xs{i}", [128, E], F32) for i in range(2)]
    xlb = [X.bsb(f"xlb{i}", [128, E], BF16) for i in range(2)]
    cols = X.cols
    dsc = X.dsc

    s_x = [X.sem("s_x0"), X.sem("s_x1")]
    s_tp = X.sem("s_tp")

    strips = [(3 + 128 * i, 128, 3 + 128 * i) for i in range(16)] + [(0, HALO, 0)]
    X.NS0 = len(strips)

    @block.sync
    def _(sy):
        for i, (r0, nr, c0) in enumerate(strips):
            if i >= 2:
                sy.wait_ge(X.vs.h, ck(f"xl{i-2}"))
                sy.wait_ge(X.ss.h, ck(f"sq{i-2}"))
            sy.dma_start(out=xs[i % 2][0:nr, :], in_=X.x_d[r0:r0 + nr, :]).then_inc(s_x[i % 2].h, 16)
            s_x[i % 2].inc(16)

    @block.scalar
    def _(sc):
        S_ = X.SC(sc)
        S_.wait(X.vs.h, ck("misc"))
        for i, (r0, nr, c0) in enumerate(strips):
            w = 128 if nr == 128 else 4
            ww = 128 if nr == 128 else HALO
            S_.wait(s_x[i % 2].h, 16 * (i // 2 + 1))
            if i >= 1:
                S_.wait(X.vs.h, ck(f"st{i-1}"))
            S_.op("activation", out=dsc[0:nr, :], in_=xs[i % 2][0:nr, :], func=AF.Square,
                  accum_out=cols[0:nr, 1:2])
            S_.ck(f"sq{i}")
            S_.wait(X.vs.h, ck(f"st{i}"))
            S_.op("activation", out=cols[0:nr, 6:7], in_=cols[0:nr, 5:6], func=AF.Ln,
                  bias=X.fconst[0:nr, 0:1], scale=1.0)
            S_.op("activation", out=cols[0:nr, 7:8], in_=cols[0:nr, 6:7], func=AF.Exp,
                  scale=-0.5)
            S_.ck(f"rs{i}")
            for half in range(2):
                bidx = 2 * i + half
                S_.wait(s_tp.h, bidx + 1)
                src = X.ps_t[:, 0:8 * w].rearrange("p (d s) -> p d s", s=w)[:, :, 0:ww]
                dst = X.xlnT[:].rearrange("p (k s) -> p k s", s=SH)[:, half * 8:half * 8 + 8, c0:c0 + ww]
                S_.op("copy", out=dst, in_=src)
                S_.ck(f"te0_{bidx}")

    @block.vector
    def _(v):
        V = X.VC(v)
        for i, (r0, nr, c0) in enumerate(strips):
            V.wait(s_x[i % 2].h, 16 * (i // 2 + 1))
            V.op("reduce_sum", out=cols[0:nr, 0:1], in_=xs[i % 2][0:nr, :], axis=AX.X)
            V.wait(X.ss.h, ck(f"sq{i}"))
            V.op("tensor_scalar_mul", out=cols[0:nr, 2:3], in0=cols[0:nr, 0:1], scalar1=1.0 / E)
            V.op("tensor_scalar_mul", out=cols[0:nr, 3:4], in0=cols[0:nr, 1:2], scalar1=1.0 / E)
            V.op("tensor_mul", out=cols[0:nr, 4:5], in0=cols[0:nr, 2:3], in1=cols[0:nr, 2:3])
            V.op("tensor_sub", out=cols[0:nr, 5:6], in0=cols[0:nr, 3:4], in1=cols[0:nr, 4:5])
            V.ck(f"st{i}")
            V.wait(X.ss.h, ck(f"rs{i}"))
            V.op("scalar_tensor_tensor", out=cols[0:nr, 8:9], in0=cols[0:nr, 2:3], scalar=-1.0,
                 in1=cols[0:nr, 7:8], op0=ALU.mult, op1=ALU.mult)
            if i >= 2:
                V.wait(s_tp.h, 2 * (i - 1))
            V.op("tensor_scalar", out=xlb[i % 2][0:nr, :], in0=xs[i % 2][0:nr, :],
                 scalar1=cols[0:nr, 7:8], scalar2=cols[0:nr, 8:9],
                 op0=ALU.mult, op1=ALU.add)
            V.ck(f"xl{i}")

    @block.tensor
    def _(t):
        t.wait_ge(X.s_c.h, X.s_c.n)
        for i, (r0, nr, c0) in enumerate(strips):
            w = 128 if nr == 128 else 4
            ww = 128 if nr == 128 else HALO
            t.wait_ge(X.vs.h, ck(f"xl{i}"))
            for half in range(2):
                bidx = 2 * i + half
                if bidx >= 1:
                    t.wait_ge(X.ss.h, ck(f"te0_{bidx-1}"))
                for d in range(8):
                    dt_i = half * 8 + d
                    ti = t.transpose(out=X.ps_t[:, d * w:d * w + ww],
                                     in_=xlb[i % 2][0:nr, dt_i * 128:(dt_i + 1) * 128],
                                     identity=X.identb[0:nr, 0:nr])
                ti.then_inc(s_tp.h, 1)
                s_tp.inc()

    X.final.append(("ck", X.ss, "te0_33"))


def _stage1(X):
    block = X.block
    ck = X.ck
    xk = X.xk
    wbuf = X.wbuf; hpre = X.hpre; cpre = X.cpre; thb = X.thb
    ps_a, ps_b, ps_h = X.ps_a, X.ps_b, X.ps_h

    silb = X.bsb("silb", [128, S], BF16)

    s_wl = [X.sem("s_wl0"), X.sem("s_wl1")]
    s_mm = X.sem("s_mm")
    s_cvm = X.sem("s_cvm")
    s_sp = X.sem("s_sp")
    s_at = X.sem("s_at")
    X.s_sp = s_sp
    X.s_mm1 = s_mm

    tiles1 = list(range(64, 72))

    @block.gpsimd
    def _(g):
        g.wait_ge(X.ss.h, ck("te0_33"))
        for j, tt in enumerate(tiles1 + [72]):
            if j >= 2:
                g.wait_ge(s_mm.h, j - 1)
            w = 64 if tt == 72 else 128
            g.dma_start(out=X.wbuf[j % 2][:, 0:NKT * w].rearrange("p (k m) -> p k m", k=NKT),
                        in_=X.win_d[tt, :, :].rearrange("p (k m) -> p k m", k=NKT)[:, :, 0:w]
                        ).then_inc(s_wl[j % 2].h, 16)
            s_wl[j % 2].inc(16)

    @block.tensor
    def _(t):
        for j, tt in enumerate(tiles1 + [72]):
            w = 64 if tt == 72 else 128
            t.wait_ge(s_wl[j % 2].h, 16 * (j // 2 + 1))
            if 1 <= j <= 8:
                t.wait_ge(X.ss.h, ck(f"cve1_{j-1}"))
            lhs = lambda k: wbuf[j % 2][:, k * w:(k + 1) * w]
            mi = None
            for half, ps in ((0, ps_a), (1, ps_b)):
                for nb in range(2):
                    for k in range(NKT):
                        mi = t.matmul(out=ps[0:w, nb * 512:(nb + 1) * 512], lhsT=lhs(k),
                                      rhs=xk(k, 3 + half * 1024 + nb * 512, 3 + half * 1024 + (nb + 1) * 512),
                                      start=(k == 0), stop=(k == NKT - 1))
            if tt != 72:
                for k in range(NKT):
                    mi = t.matmul(out=ps_h[0:w, 0:HALO], lhsT=lhs(k), rhs=xk(k, 0, HALO),
                                  start=(k == 0), stop=(k == NKT - 1))
            mi.then_inc(s_mm.h, 1)
            s_mm.inc()
            if tt != 72:
                t.wait_ge(X.ss.h, ck(f"ev1_{j}"))
                t.wait_ge(X.vs.h, ck(f"dg1_{j}"))
                if j >= 1:
                    t.wait_ge(X.ss.h, ck(f"cve1_{j-1}"))
                for half in range(2):
                    banks = (X.ps_yd, X.ps_g) if half == 0 else (ps_a, ps_b)
                    mi = None
                    for nbl, psx in ((0, banks[0]), (1, banks[1])):
                        b0 = half * 1024 + nbl * 512
                        for k in range(4):
                            mi = t.matmul(out=psx[:, 0:512], lhsT=X.diag4[:, k * 128:(k + 1) * 128],
                                          rhs=hpre[:, b0 + k: b0 + k + 512],
                                          start=(k == 0), stop=(k == 3))
                    mi.then_inc(s_cvm.h, 1)
                    s_cvm.inc()

    @block.scalar
    def _(sc):
        S_ = X.SC(sc)
        for j, tt in enumerate(tiles1):
            S_.wait(s_mm.h, j + 1)
            if j >= 1:
                S_.wait(s_cvm.h, 2 * j)
            S_.op("copy", out=hpre[:, 0:HALO], in_=ps_h[:, 0:HALO])
            S_.op("copy", out=hpre[:, HALO:HALO + 1024], in_=ps_a[:, 0:1024])
            S_.op("copy", out=hpre[:, HALO + 1024:SH], in_=ps_b[:, 0:1024])
            S_.ck(f"ev1_{j}")
            tc = tt - 32
            if j >= 1:
                S_.wait(X.vs.h, ck(f"si1_{j-1}"))
            S_.wait(s_cvm.h, 2 * j + 1)
            S_.op("copy", out=cpre[:, 0:512], in_=X.ps_yd[:, 0:512])
            S_.op("copy", out=cpre[:, 512:1024], in_=X.ps_g[:, 0:512])
            S_.wait(s_cvm.h, 2 * j + 2)
            S_.op("copy", out=cpre[:, 1024:1536], in_=ps_a[:, 0:512])
            S_.op("copy", out=cpre[:, 1536:2048], in_=ps_b[:, 0:512])
            S_.ck(f"cve1_{j}")
            S_.op("activation", out=thb[:], in_=cpre[:], func=AF.Tanh, scale=0.5,
                  bias=X.hconvb_sb[:, tc:tc + 1])
            S_.ck(f"th1_{j}")

    @block.vector
    def _(v):
        V = X.VC(v)
        for j, tt in enumerate(tiles1):
            tc = tt - 32
            if j >= 1:
                V.wait(s_cvm.h, 2 * j)
            for k in range(4):
                V.op("tensor_scalar_mul", out=X.diag4[:, k * 128:(k + 1) * 128],
                     in0=X.identb[:], scalar1=X.convw_sb[:, tc * 4 + k:tc * 4 + k + 1])
            V.ck(f"dg1_{j}")
            V.wait(X.ss.h, ck(f"th1_{j}"))
            if j >= 1:
                V.wait(s_sp.h, 16 * j)
            V.op("tensor_scalar", out=thb[:], in0=thb[:], scalar1=0.5, scalar2=0.5,
                 op0=ALU.mult, op1=ALU.add)
            V.op("scalar_tensor_tensor", out=silb[:], in0=cpre[:], scalar=X.convb_sb[:, tc:tc + 1],
                 in1=thb[:], op0=ALU.add, op1=ALU.mult)
            V.ck(f"si1_{j}")

    @block.sync
    def _(sy):
        for j, tt in enumerate(tiles1):
            sy.wait_ge(X.vs.h, ck(f"si1_{j}"))
            dst = X.scr_b if tt < 68 else X.scr_c
            r0 = ((tt - 64) if tt < 68 else (tt - 68)) * 128
            sy.dma_start(out=dst[r0:r0 + 128, :], in_=silb[:]).then_inc(s_sp.h, 16)
            s_sp.inc(16)

    u_t = X.bsb("u_t", [NH, S], F32)
    dt_t = X.bsb("dt_t", [NH, S], F32)
    lndt_t = X.bsb("lndt_t", [NH, S], F32)
    acum_t = X.bsb("acum_t", [NH, S], F32)
    tmpa = X.bsb("tmpa", [NH, S], F32)
    sdb = X.bsb("sdb", [NH, S], BF16)
    ddb = X.bsb("ddb", [NH, S], BF16)
    z64 = X.bsb("z64", [NH, CH], F32)
    acol = X.bsb("acol", [NH, 2], F32)

    @block.vector
    def _(v):
        V = X.VC(v)
        V.wait(s_mm.h, 9)
        V.op("memset", z64[:], 0.0)
        V.op("tensor_scalar_add", out=u_t[:, 0:1024], in0=ps_a[0:NH, 0:1024], scalar1=X.dtb_sb[:])
        V.op("tensor_scalar_add", out=u_t[:, 1024:2048], in0=ps_b[0:NH, 0:1024], scalar1=X.dtb_sb[:])
        V.ck("dtu")

    @block.scalar
    def _(sc):
        S_ = X.SC(sc)
        S_.wait(X.vs.h, ck("dtu"))
        S_.op("activation", out=tmpa[:], in_=u_t[:], func=AF.Abs)
        S_.op("activation", out=tmpa[:], in_=tmpa[:], func=AF.Exp, scale=-1.0)
        S_.ck("dtexp")

    @block.vector
    def _(v):
        V = X.VC(v)
        V.wait(X.ss.h, ck("dtexp"))
        V.op("tensor_scalar_add", out=tmpa[:], in0=tmpa[:], scalar1=1.0)
        V.ck("dtp1")

    @block.scalar
    def _(sc):
        S_ = X.SC(sc)
        S_.wait(X.vs.h, ck("dtp1"))
        S_.op("activation", out=tmpa[:], in_=tmpa[:], func=AF.Ln)
        S_.op("activation", out=acol[:, 0:1], in_=X.alog_sb[:], func=AF.Exp)
        S_.ck("dtln")

    @block.vector
    def _(v):
        V = X.VC(v)
        V.wait(X.ss.h, ck("dtln"))
        V.op("tensor_scalar_max", out=u_t[:], in0=u_t[:], scalar1=0.0)
        V.op("tensor_add", out=dt_t[:], in0=tmpa[:], in1=u_t[:])
        V.op("tensor_scalar_mul", out=acol[:, 1:2], in0=acol[:, 0:1], scalar1=-1.0)
        V.op("tensor_scalar_mul", out=tmpa[:], in0=dt_t[:], scalar1=acol[:, 1:2])
        for c in range(NCHUNK):
            V.op("tensor_tensor_scan", out=acum_t[:, c * CH:(c + 1) * CH],
                 data0=tmpa[:, c * CH:(c + 1) * CH], data1=z64[:],
                 initial=0.0, op0=ALU.add, op1=ALU.add)
        V.ck("acum")

    @block.scalar
    def _(sc):
        S_ = X.SC(sc)
        S_.wait(X.vs.h, ck("acum"))
        S_.op("activation", out=lndt_t[:], in_=dt_t[:], func=AF.Ln)
        S_.op("activation", out=sdb[:], in_=acum_t[:], func=AF.Exp)
        S_.ck("sdb")
        for c in range(NCHUNK):
            S_.op("activation", out=tmpa[:, c * CH:(c + 1) * CH], in_=acum_t[:, c * CH:(c + 1) * CH],
                  func=AF.Exp, scale=-1.0,
                  bias=acum_t[:, c * CH + CH - 1: c * CH + CH])
        S_.ck("decay")

    @block.vector
    def _(v):
        V = X.VC(v)
        V.wait(X.ss.h, ck("decay"))
        V.op("tensor_mul", out=ddb[:], in0=dt_t[:], in1=tmpa[:])
        V.ck("ddb")
        V.op("tensor_sub", out=tmpa[:], in0=lndt_t[:], in1=acum_t[:])
        V.ck("biasf")

    @block.tensor
    def _(t):
        t.wait_ge(X.vs.h, ck("biasf"))
        t.wait_ge(X.ss.h, ck("ev1_7"))
        for c in range(NCHUNK):
            if c >= 1:
                t.wait_ge(X.ss.h, ck(f"ae{c-1}"))
            ti = None
            for st in range(2):
                ti = t.transpose(out=ps_h[:, st * 64:(st + 1) * 64],
                                 in_=tmpa[:, c * CH + st * 128: c * CH + (st + 1) * 128],
                                 identity=X.identf[0:NH, 0:NH])
            ti.then_inc(s_at.h, 1)
            s_at.inc()

    @block.scalar
    def _(sc):
        S_ = X.SC(sc)
        for c in range(NCHUNK):
            S_.wait(s_at.h, c + 1)
            S_.op("copy", out=X.biasDT[:, c * 128:(c + 1) * 128], in_=ps_h[:, 0:128])
            S_.ck(f"ae{c}")

    @block.sync
    def _(sy):
        sy.wait_ge(X.ss.h, ck("sdb"))
        sy.dma_start(out=X.scr_sd[:], in_=sdb[:]).then_inc(s_sp.h, 16); s_sp.inc(16)
        sy.wait_ge(X.vs.h, ck("biasf"))
        sy.dma_start(out=X.scr_acum[:], in_=acum_t[:]).then_inc(s_sp.h, 16); s_sp.inc(16)
        sy.dma_start(out=X.scr_dtd[:], in_=ddb[:]).then_inc(s_sp.h, 16); s_sp.inc(16)

    X.final.append((s_sp, s_sp.n))


def _stage2(X):
    block = X.block
    ck = X.ck
    asb = X.asb
    xk = X.xk
    wbuf, wgbuf = X.wbuf, X.wgbuf
    hpre, cpre, thb = X.hpre, X.cpre, X.thb
    dsc = X.dsc
    ps_a, ps_b, ps_h, ps_t, ps_yd, ps_g = X.ps_a, X.ps_b, X.ps_h, X.ps_t, X.ps_yd, X.ps_g

    bsil = asb("bsil", [128, S], BF16)
    csil = asb("csil", [128, S], BF16)
    gcache = asb("gcache", [128, 8192], BF16)
    acbc = [asb(f"acbc{i}", [128, S], F32) for i in range(2)]
    sdp = [asb(f"sdp{i}", [128, S], BF16) for i in range(2)]
    ddp = [asb(f"ddp{i}", [128, S], BF16) for i in range(2)]
    mbuf = [asb(f"mbuf{i}", [128, 4096], BF16) for i in range(2)]
    bg = [mbuf[0][0:NH, 0:S], mbuf[1][0:NH, 0:S]]
    cg = [mbuf[0][0:NH, S:2 * S], mbuf[1][0:NH, S:2 * S]]
    htall = asb("htall", [128, 2048], BF16)
    hpair = asb("hpair", [128, S], BF16)
    gsb = asb("gsb2", [128, S], BF16)
    ypair = [asb(f"ypair{i}", [128, S], BF16) for i in range(2)]
    stcol = asb("stcol", [128, 8], F32)

    s2_bc = X.sem("s2_bc")
    s2_wh = [X.sem("s2_wh0"), X.sem("s2_wh1")]
    s2_wg = X.sem("s2_wg")
    s2_ab = [X.sem("s2_ab0"), X.sem("s2_ab1")]
    s2_sd = [X.sem("s2_sd0"), X.sem("s2_sd1")]
    s2_spl = [X.sem("s2_spl0"), X.sem("s2_spl1")]
    s2_mm = X.sem("s2_mm"); s2_gmm = X.sem("s2_gmm"); s2_cvm = X.sem("s2_cvm")
    s2_tp = X.sem("s2_tp"); s2_yd = X.sem("s2_yd"); s2_gq = X.sem("s2_gq")

    pairs = [ph + 4 * j for ph in range(4) for j in range(8)]
    NP = len(pairs)
    X.NP = NP
    X.s2_spl = s2_spl

    @block.gpsimd
    def _(g):
        g.wait_ge(X.s_mm1.h, 9)
        for idx, t_ in enumerate(pairs):
            if idx >= 2:
                g.wait_ge(s2_gmm.h, idx - 1)
            g.dma_start(out=wbuf[idx % 2][:],
                        in_=X.win_d[32 + t_, :, :]).then_inc(s2_wh[idx % 2].h, 16)
            g.wait_ge(s2_gmm.h, idx)
            g.dma_start(out=wgbuf[0][:],
                        in_=X.win_d[t_, :, :]).then_inc(s2_wg.h, 16)
            s2_wh[idx % 2].inc(16); s2_wg.inc(16)

    @block.sync
    def _(sy):
        sy.wait_ge(X.s_sp.h, X.s_sp.n)
        sy.wait_ge(X.ss.h, ck("ae7"))
        for idx, t_ in enumerate(pairs):
            ph = t_ % 4
            h1, h2 = 2 * t_, 2 * t_ + 1
            if idx % 8 == 0:
                if idx > 0:
                    sy.wait_ge(X.vs.h, ck(f"u{idx-1}"))
                    sy.wait_ge(s2_yd.h, 8 * idx)
                sy.dma_start(out=bsil[:], in_=X.scr_b[ph * 128:(ph + 1) * 128, :]).then_inc(s2_bc.h, 16)
                sy.dma_start(out=csil[:], in_=X.scr_c[ph * 128:(ph + 1) * 128, :]).then_inc(s2_bc.h, 16)
                sy.dma_start(out=bg[0], in_=X.scr_b[ph * 128:ph * 128 + 64, :]).then_inc(s2_bc.h, 16)
                sy.dma_start(out=bg[1], in_=X.scr_b[ph * 128 + 64:(ph + 1) * 128, :]).then_inc(s2_bc.h, 16)
                sy.dma_start(out=cg[0], in_=X.scr_c[ph * 128:ph * 128 + 64, :]).then_inc(s2_bc.h, 16)
                sy.dma_start(out=cg[1], in_=X.scr_c[ph * 128 + 64:(ph + 1) * 128, :]).then_inc(s2_bc.h, 16)
                s2_bc.inc(96)
            for hh in range(2):
                hidx = 2 * idx + hh
                hab = h1 + hh
                if hidx >= 2:
                    sy.wait_ge(X.vs.h, ck(f"d{hidx-2}"))
                sy.dma_start(out=acbc[hidx % 2][:],
                             in_=X.scr_acum[hab:hab + 1, :].partition_broadcast(128)).then_inc(s2_ab[hidx % 2].h, 16)
                s2_ab[hidx % 2].inc(16)
            if idx >= 2:
                sy.wait_ge(X.vs.h, ck(f"u{idx-2}"))
            sy.dma_start(out=sdp[idx % 2][0:64, :], in_=X.scr_sd[h1:h1 + 1, :].partition_broadcast(64)).then_inc(s2_sd[idx % 2].h, 16)
            sy.dma_start(out=sdp[idx % 2][64:128, :], in_=X.scr_sd[h2:h2 + 1, :].partition_broadcast(64)).then_inc(s2_sd[idx % 2].h, 16)
            sy.dma_start(out=ddp[idx % 2][0:64, :], in_=X.scr_dtd[h1:h1 + 1, :].partition_broadcast(64)).then_inc(s2_sd[idx % 2].h, 16)
            sy.dma_start(out=ddp[idx % 2][64:128, :], in_=X.scr_dtd[h2:h2 + 1, :].partition_broadcast(64)).then_inc(s2_sd[idx % 2].h, 16)
            s2_sd[idx % 2].inc(64)
            if idx >= 1:
                tp_ = pairs[idx - 1]
                sy.wait_ge(X.vs.h, ck(f"yg{idx-1}"))
                sy.dma_start(out=X.scr_yg[tp_ * 128: tp_ * 128 + 128, :],
                             in_=ypair[(idx - 1) % 2][:]).then_inc(s2_spl[(idx - 1) % 2].h, 16)
                s2_spl[(idx - 1) % 2].inc(16)
        sy.wait_ge(X.vs.h, ck(f"yg{NP-1}"))
        sy.dma_start(out=X.scr_yg[pairs[-1] * 128: pairs[-1] * 128 + 128, :],
                     in_=ypair[(NP - 1) % 2][:]).then_inc(s2_spl[(NP - 1) % 2].h, 16)
        s2_spl[(NP - 1) % 2].inc(16)

    @block.tensor
    def _(t):
        t.wait_ge(X.vs.h, ck("dtu"))
        for idx, t_ in enumerate(pairs):
            π = idx % 2
            p_ = idx // 8
            if idx % 8 == 0:
                t.wait_ge(s2_bc.h, 96 * (p_ + 1))
                if idx > 0:
                    t.wait_ge(X.vs.h, ck(f"ya{idx-1}"))
                unit = 0
                for half in range(2):
                    for c in range(NCHUNK):
                        psx = ps_g if unit % 2 == 0 else ps_yd
                        if unit >= 2:
                            t.wait_ge(X.vs.h, ck(f"gm{p_}_{unit-2}"))
                        mi = None
                        for st in range(2):
                            mi = t.matmul(out=psx[:, st * 256:(st + 1) * 256],
                                          lhsT=mbuf[half][0:64, c * CH + st * 128: c * CH + (st + 1) * 128],
                                          rhs=mbuf[half][0:64, S + c * CH: S + (c + 1) * CH],
                                          start=True, stop=True)
                        mi.then_inc(s2_gq.h, 1); s2_gq.inc()
                        unit += 1
            t.wait_ge(s2_wh[π].h, 16 * (idx // 2 + 1))
            if idx >= 1:
                t.wait_ge(X.ss.h, ck(f"gev{idx-1}"))
                t.wait_ge(X.ss.h, ck(f"ev2_{idx-1}"))
            else:
                t.wait_ge(s2_bc.h, 96)
            lhs = lambda k: wbuf[π][:, k * 128:(k + 1) * 128]
            for half, ps in ((0, ps_a), (1, ps_b)):
                for nb in range(2):
                    for k in range(NKT):
                        t.matmul(out=ps[:, nb * 512:(nb + 1) * 512], lhsT=lhs(k),
                                 rhs=xk(k, 3 + half * 1024 + nb * 512, 3 + half * 1024 + (nb + 1) * 512),
                                 start=(k == 0), stop=(k == NKT - 1))
            mi = None
            for k in range(NKT):
                mi = t.matmul(out=ps_h[:, 0:HALO], lhsT=lhs(k), rhs=xk(k, 0, HALO),
                              start=(k == 0), stop=(k == NKT - 1))
            mi.then_inc(s2_mm.h, 1); s2_mm.inc()
            t.wait_ge(X.ss.h, ck(f"ev2_{idx}"))
            t.wait_ge(X.vs.h, ck(f"dg2_{idx}"))
            if idx % 8 == 0:
                t.wait_ge(X.vs.h, ck(f"gm{p_}_15"))
            elif idx >= 1:
                t.wait_ge(X.vs.h, ck(f"ya{idx-1}"))
            if idx >= 1:
                t.wait_ge(X.ss.h, ck(f"cve2_{idx-1}"))
            for half in range(2):
                banks = (ps_yd, ps_g) if half == 0 else (ps_a, ps_b)
                mi = None
                for nbl, psx in ((0, banks[0]), (1, banks[1])):
                    b0 = half * 1024 + nbl * 512
                    for k in range(4):
                        mi = t.matmul(out=psx[:, 0:512], lhsT=X.diag4[:, k * 128:(k + 1) * 128],
                                      rhs=hpre[:, b0 + k: b0 + k + 512],
                                      start=(k == 0), stop=(k == 3))
                mi.then_inc(s2_cvm.h, 1)
                s2_cvm.inc()
            t.wait_ge(s2_wg.h, 16 * (idx + 1))
            t.wait_ge(X.ss.h, ck(f"cve2_{idx}"))
            lhg = lambda k: wgbuf[0][:, k * 128:(k + 1) * 128]
            mi = None
            for half, ps in ((0, ps_a), (1, ps_b)):
                for nb in range(2):
                    for k in range(NKT):
                        mi = t.matmul(out=ps[:, nb * 512:(nb + 1) * 512], lhsT=lhg(k),
                                      rhs=xk(k, 3 + half * 1024 + nb * 512, 3 + half * 1024 + (nb + 1) * 512),
                                      start=(k == 0), stop=(k == NKT - 1))
            mi.then_inc(s2_gmm.h, 1); s2_gmm.inc()
            t.wait_ge(X.vs.h, ck(f"h{idx}"))
            for b in range(2):
                bidx = 2 * idx + b
                if bidx >= 1:
                    pi, pb = (bidx - 1) // 2, (bidx - 1) % 2
                    t.wait_ge(X.ss.h, ck(f"te2_{pi}_{pb}"))
                ti = None
                for cc in range(4):
                    c = b * 4 + cc
                    for st in range(2):
                        ti = t.transpose(out=ps_t[:, (cc * 2 + st) * 128:(cc * 2 + st + 1) * 128],
                                         in_=hpair[:, c * CH + st * 128: c * CH + (st + 1) * 128],
                                         identity=X.identb[:])
                ti.then_inc(s2_tp.h, 1); s2_tp.inc()
            t.wait_ge(X.vs.h, ck(f"m{2 * idx + 1}"))
            t.wait_ge(X.ss.h, ck(f"te2_{idx}_1"))
            if idx >= 1:
                t.wait_ge(X.vs.h, ck(f"ya{idx-1}"))
            for c in range(NCHUNK):
                if c >= 2:
                    t.wait_ge(X.vs.h, ck(f"ya{idx}_{c-2}"))
                psy = ps_yd if c % 2 == 0 else ps_g
                mi = None
                for hh in range(2):
                    for st in range(2):
                        mi = t.matmul(out=psy[hh * 64:(hh + 1) * 64, 0:256],
                                      lhsT=htall[:, (c * 2 + st) * 128 + hh * 64: (c * 2 + st) * 128 + (hh + 1) * 64],
                                      rhs=mbuf[hh][:, (c * 2 + st) * 256:(c * 2 + st + 1) * 256],
                                      start=(st == 0), stop=(st == 1),
                                      tile_position=(0, hh * 64))
                mi.then_inc(s2_yd.h, 1); s2_yd.inc()

    @block.scalar
    def _(sc):
        S_ = X.SC(sc)
        for idx, t_ in enumerate(pairs):
            tc = t_
            S_.wait(s2_mm.h, idx + 1)
            if idx >= 1:
                S_.wait(s2_cvm.h, 2 * idx)
            S_.op("copy", out=hpre[:, 0:HALO], in_=ps_h[:, 0:HALO])
            S_.op("copy", out=hpre[:, HALO:HALO + 1024], in_=ps_a[:, 0:1024])
            S_.op("copy", out=hpre[:, HALO + 1024:SH], in_=ps_b[:, 0:1024])
            S_.ck(f"ev2_{idx}")
            if idx >= 1:
                S_.wait(X.vs.h, ck(f"u{idx-1}"))
                S_.wait(X.vs.h, ck(f"yg{idx-1}"))
            S_.wait(s2_cvm.h, 2 * idx + 1)
            S_.op("copy", out=cpre[:, 0:512], in_=ps_yd[:, 0:512])
            S_.op("copy", out=cpre[:, 512:1024], in_=ps_g[:, 0:512])
            S_.wait(s2_cvm.h, 2 * idx + 2)
            S_.op("copy", out=cpre[:, 1024:1536], in_=ps_a[:, 0:512])
            S_.op("copy", out=cpre[:, 1536:2048], in_=ps_b[:, 0:512])
            S_.ck(f"cve2_{idx}")
            S_.op("activation", out=thb[:], in_=cpre[:], func=AF.Tanh, scale=0.5,
                  bias=X.hconvb_sb[:, tc:tc + 1])
            S_.ck(f"th2_{idx}")
            for hh in range(2):
                hidx = 2 * idx + hh
                if idx >= 1 and hh == 0:
                    S_.wait(s2_yd.h, 8 * idx)
                if hh == 0:
                    S_.wait(s2_gq.h, 16 * (idx // 8 + 1))
                for half in range(2):
                    S_.wait(X.vs.h, ck(f"dh{hidx}_{half}"))
                    S_.op("activation", out=mbuf[hh][:, half * 2048:(half + 1) * 2048],
                          in_=dsc[:], func=AF.Exp)
                    S_.ck(f"e{hidx}_{half}")
            for b in range(2):
                S_.wait(s2_tp.h, 2 * idx + b + 1)
                S_.op("copy", out=htall[:, b * 1024:(b + 1) * 1024], in_=ps_t[:, 0:1024])
                S_.ck(f"te2_{idx}_{b}")
            S_.wait(s2_gmm.h, idx + 1)
            if idx >= 1:
                S_.wait(X.vs.h, ck(f"yg{idx-1}"))
            S_.op("copy", out=gsb[:, 0:1024], in_=ps_a[:, 0:1024])
            S_.op("copy", out=gsb[:, 1024:2048], in_=ps_b[:, 0:1024])
            S_.ck(f"gev{idx}")
            S_.wait(X.vs.h, ck(f"u{idx}"))
            S_.op("activation", out=thb[:, 0:1024], in_=gsb[:, 0:1024], func=AF.Tanh, scale=0.5)
            S_.op("activation", out=thb[:, 1024:2048], in_=gsb[:, 1024:2048], func=AF.Tanh, scale=0.5)
            S_.ck(f"gth{idx}")

    @block.vector
    def _(v):
        V = X.VC(v)
        for idx, t_ in enumerate(pairs):
            π = idx % 2
            p_ = idx // 8
            tc = t_
            if idx % 8 == 0:
                for unit in range(16):
                    psx = ps_g if unit % 2 == 0 else ps_yd
                    half, c = unit // 8, unit % 8
                    V.wait(s2_gq.h, 16 * p_ + unit + 1)
                    V.op("tensor_mul", out=gcache[:, half * 4096 + c * 512: half * 4096 + (c + 1) * 512],
                         in0=psx[:, 0:512], in1=X.mask_sb[:])
                    V.ck(f"gm{p_}_{unit}")
            if idx >= 1:
                V.wait(s2_cvm.h, 2 * idx)
            for k in range(4):
                V.op("tensor_scalar_mul", out=X.diag4[:, k * 128:(k + 1) * 128],
                     in0=X.identb[:], scalar1=X.convw_sb[:, tc * 4 + k:tc * 4 + k + 1])
            V.ck(f"dg2_{idx}")
            for hh in range(2):
                hidx = 2 * idx + hh
                hab = 2 * t_ + hh
                V.wait(s2_ab[hidx % 2].h, 16 * (hidx // 2 + 1))
                for half in range(2):
                    if hidx > 0 or half > 0:
                        pr_h, pr_half = (hidx, half - 1) if half > 0 else (hidx - 1, 1)
                        V.wait(X.ss.h, ck(f"e{pr_h}_{pr_half}"))
                    in0 = acbc[hidx % 2][:, half * 1024:(half + 1) * 1024] \
                        .rearrange("p (c l) -> p c l", l=CH)
                    for st in range(2):
                        in1 = X.biasDT[:].rearrange("p (c st h) -> p c st h", st=2, h=64) \
                            [:, half * 4:(half + 1) * 4, st, hab:hab + 1] \
                            .to_broadcast([128, 4, CH])
                        outD = dsc[:].rearrange("p (c st l) -> p c st l", st=2, l=CH)[:, :, st, :]
                        V.op("scalar_tensor_tensor", out=outD, in0=in0, scalar=1.0, in1=in1,
                             op0=ALU.mult, op1=ALU.add)
                    V.op("tensor_scalar_min", out=dsc[:], in0=dsc[:], scalar1=80.0)
                    V.ck(f"dh{hidx}_{half}")
                    if half == 1:
                        V.ck(f"d{hidx}")
                V.wait(X.ss.h, ck(f"e{hidx}_1"))
                V.op("tensor_mul", out=mbuf[hh][:], in0=mbuf[hh][:],
                     in1=gcache[:, hh * 4096:(hh + 1) * 4096])
                V.ck(f"m{hidx}")
            V.wait(X.ss.h, ck(f"th2_{idx}"))
            if idx >= 1:
                V.wait(s2_tp.h, 2 * idx)
            V.op("tensor_scalar", out=thb[:], in0=thb[:], scalar1=0.5, scalar2=0.5,
                 op0=ALU.mult, op1=ALU.add)
            V.op("scalar_tensor_tensor", out=hpair[:], in0=cpre[:], scalar=X.convb_sb[:, tc:tc + 1],
                 in1=thb[:], op0=ALU.add, op1=ALU.mult)
            V.ck(f"h{idx}")
            V.wait(s2_sd[π].h, 64 * (idx // 2 + 1))
            V.op("tensor_mul", out=thb[:], in0=bsil[:], in1=hpair[:])
            V.op("tensor_mul", out=thb[:], in0=thb[:], in1=ddp[π][:])
            V.op("reduce_sum", out=stcol[:].rearrange("p (c k) -> p c k", k=1),
                 in_=thb[:].rearrange("p (c k) -> p c k", k=CH), axis=AX.X)
            V.op("tensor_tensor", out=cpre[:].rearrange("p (c k) -> p c k", k=CH),
                 in0=csil[:].rearrange("p (c k) -> p c k", k=CH),
                 in1=stcol[:].rearrange("p (c k) -> p c k", k=1).to_broadcast([128, NCHUNK, CH]),
                 op=ALU.mult)
            V.op("tensor_mul", out=cpre[:], in0=cpre[:], in1=sdp[π][:])
            if idx >= 2:
                V.wait(s2_spl[π].h, 16 * (idx // 2))
            V.op("scalar_tensor_tensor", out=ypair[π][:], in0=hpair[:], scalar=X.dcol_sb[:, t_:t_ + 1],
                 in1=cpre[:], op0=ALU.mult, op1=ALU.add)
            V.ck(f"u{idx}")
            for c in range(NCHUNK):
                V.wait(s2_yd.h, 8 * idx + c + 1)
                psy = ps_yd if c % 2 == 0 else ps_g
                V.op("tensor_add", out=ypair[π][:, c * CH:(c + 1) * CH],
                     in0=ypair[π][:, c * CH:(c + 1) * CH], in1=psy[:, 0:256])
                V.ck(f"ya{idx}_{c}")
            V.ck(f"ya{idx}")
            V.wait(X.ss.h, ck(f"gth{idx}"))
            V.op("tensor_scalar", out=thb[:], in0=thb[:], scalar1=0.5, scalar2=0.5,
                 op0=ALU.mult, op1=ALU.add)
            V.op("tensor_mul", out=thb[:], in0=thb[:], in1=gsb[:])
            V.op("tensor_mul", out=ypair[π][:], in0=ypair[π][:], in1=thb[:])
            V.op("tensor_mul", out=dsc[:], in0=ypair[π][:], in1=ypair[π][:])
            V.op("tensor_add", out=X.acc[:], in0=X.acc[:], in1=dsc[:])
            V.ck(f"yg{idx}")

    X.final.append((s2_spl[0], s2_spl[0].n))
    X.final.append((s2_spl[1], s2_spl[1].n))


def _stage3(X):
    block = X.block
    ck = X.ck
    csb, cps = X.csb, X.cps
    acc = X.acc
    NP = X.NP
    s2_spl = X.s2_spl

    wout_sb = csb("wout_sb", [128, 32 * 2048], BF16)
    ygst = [csb(f"ygst{i}", [128, 4096], BF16) for i in range(2)]

    xres = [csb(f"xres{i}", [128, 2048], F32) for i in range(2)]
    o1b = [csb(f"o1b{i}", [128, 2048], BF16) for i in range(2)]
    x2b = [csb(f"x2b{i}", [128, 2048], BF16) for i in range(2)]
    x2st = csb("x2st", [128, 2048], BF16)
    cols3 = csb("cols3", [128, 16], F32)
    wp1 = [csb(f"wp1_{i}", [128, 1920], F32) for i in range(2)]
    rcol = csb("rcol", [128, 16], F32)

    p1all = []
    for r in range(16):
        for c0, nc_ in ((0, 1920), (1920, 1920), (3840, 1792)):
            p1all.append((X.w1_d, r, c0, nc_, r, c0 > 0))
    for r in range(16):
        for c0, nc_ in ((0, 1920), (1920, 1920), (3840, 1792)):
            p1all.append((X.w2_d, r, c0, nc_, 16 + r, c0 > 0))
    for r in range(44):
        p1all.append((X.w3_d, r, 0, 2048, r, False))
    X.p1all = p1all
    NS3 = 96
    X.NS3 = NS3

    ps_o = cps("ps_o", [128, 2048], F32)
    ps_t3 = cps("ps_t3", [128, 1024], BF16)
    ps_v = cps("ps_v", [128, 512], F32)

    s3_w = [X.sem(f"s3_w{q}") for q in range(8)]
    s3_p1 = [X.sem("s3_p10"), X.sem("s3_p11")]
    s3_gt = X.sem("s3_gt"); s3_gb = X.sem("s3_gb"); s3_gw = X.sem("s3_gw")
    s3_yl = [X.sem("s3_yl0"), X.sem("s3_yl1")]
    s3_xl = [X.sem("s3_xl0"), X.sem("s3_xl1")]
    s3_vm = X.sem("s3_vm"); s3_rb = X.sem("s3_rb"); s3_mm = X.sem("s3_mm")
    s3_tp = X.sem("s3_tp")
    s3_os = [X.sem("s3_os0"), X.sem("s3_os1")]
    s3_xs = X.sem("s3_xs")
    X.s3_os = s3_os; X.s3_xs = s3_xs

    NM = 16

    @block.gpsimd
    def _(g):
        g.wait_ge(X.vs.h, ck(f"yg{NP-1}"))
        g.wait_ge(s2_spl[0].h, s2_spl[0].n)
        g.wait_ge(s2_spl[1].h, s2_spl[1].n)
        for q in range(8):
            g.dma_start(out=wout_sb[:, q * 8192:(q + 1) * 8192],
                        in_=X.wout_d[:, q * 8192:(q + 1) * 8192]
                        ).then_inc(s3_w[q].h, 16)
            s3_w[q].inc(16)
        for i in range(NS3):
            wd, r, c0, nc_, pc, accf = p1all[i]
            if i >= 2:
                g.wait_ge(X.vs.h, ck(f"p4_{i-2}"))
            g.dma_start(out=wp1[i % 2][:, 0:nc_], in_=wd[r * 128:(r + 1) * 128, c0:c0 + nc_]
                        ).then_inc(s3_p1[i % 2].h, 16)
            s3_p1[i % 2].inc(16)

    @block.tensor
    def _(t):
        t.wait_ge(X.vs.h, ck(f"yg{NP-1}"))
        for q in range(4):
            if q >= 1:
                t.wait_ge(X.ss.h, ck(f"ve{q-1}"))
            t.matmul(out=ps_v[0:1, 0:512], lhsT=X.ones_sb[:], rhs=acc[:, q * 512:(q + 1) * 512],
                     start=True, stop=True).then_inc(s3_vm.h, 1)
            s3_vm.inc()
        t.wait_ge(X.ss.h, ck("rstd"))
        ti = None
        for m in range(NM):
            ti = t.matmul(out=ps_v[:, m:m + 1], lhsT=acc[0:1, m * 128:(m + 1) * 128],
                          rhs=X.identf[0:1, 0:1], start=True, stop=True)
        ti.then_inc(s3_vm.h, 1); s3_vm.inc()
        def x2trans(m):
            t.wait_ge(X.vs.h, ck(f"x2_{m}"))
            for b in range(2):
                bidx = 2 * m + b
                if bidx >= 1:
                    pi, pb = (bidx - 1) // 2, (bidx - 1) % 2
                    t.wait_ge(X.ss.h, ck(f"te3_{pi}_{pb}"))
                ti = None
                for d in range(8):
                    dt_i = b * 8 + d
                    ti = t.transpose(out=ps_t3[:, d * 128:(d + 1) * 128],
                                     in_=x2b[m % 2][:, dt_i * 128:(dt_i + 1) * 128],
                                     identity=X.identb[:])
                ti.then_inc(s3_tp.h, 1); s3_tp.inc()
        for m in range(NM):
            t.wait_ge(s3_yl[m % 2].h, 16 * (m // 2 + 1))
            t.wait_ge(X.vs.h, ck("wn7"))
            if m >= 1:
                t.wait_ge(X.vs.h, ck(f"o1_{m-1}"))
            mi = None
            for nb in range(4):
                for k in range(32):
                    mi = t.matmul(out=ps_o[:, nb * 512:(nb + 1) * 512],
                                  lhsT=ygst[m % 2][:, k * 128:(k + 1) * 128],
                                  rhs=wout_sb[:, k * 2048 + nb * 512: k * 2048 + (nb + 1) * 512],
                                  start=(k == 0), stop=(k == 31))
            mi.then_inc(s3_mm.h, 1); s3_mm.inc()
            if m >= 1:
                x2trans(m - 1)
        x2trans(NM - 1)
        t.wait_ge(X.vs.h, ck("gv4_0"))
        t.wait_ge(X.ss.h, ck("rce"))
        t.matmul(out=ps_v[0:1, 0:1], lhsT=X.ones_sb[:], rhs=X.parts[:, 48:49],
                 start=True, stop=True).then_inc(s3_gt.h, 1)
        s3_gt.inc()
        t.wait_ge(X.vs.h, ck("gv4_1"))
        t.wait_ge(X.ss.h, ck("ge4_0"))
        t.matmul(out=ps_v[0:1, 0:1], lhsT=X.ones_sb[:], rhs=X.parts[:, 49:50],
                 start=True, stop=True).then_inc(s3_gt.h, 1)
        s3_gt.inc()

    @block.scalar
    def _(sc):
        S_ = X.SC(sc)
        for q in range(4):
            S_.wait(s3_vm.h, q + 1)
            S_.op("copy", out=acc[0:1, q * 512:(q + 1) * 512], in_=ps_v[0:1, 0:512])
            S_.ck(f"ve{q}")
        S_.op("activation", out=acc[0:1, :], in_=acc[0:1, :], func=AF.Ln, scale=1.0 / INTER, bias=X.fconst[0:1, 1:2])
        S_.op("activation", out=acc[0:1, :], in_=acc[0:1, :], func=AF.Exp, scale=-0.5)
        S_.ck("rstd")
        S_.wait(s3_vm.h, 5)
        S_.op("copy", out=rcol[:], in_=ps_v[:, 0:16])
        S_.ck("rce")
        for m in range(NM):
            S_.wait(X.vs.h, ck(f"o1_{m}"))
            if m >= 1:
                S_.wait(X.vs.h, ck(f"st3_{m-1}"))
            S_.op("activation", out=xres[m % 2][:], in_=o1b[m % 2][:], func=AF.Square,
                  accum_out=cols3[:, 1:2])
            S_.ck(f"sq3_{m}")
            S_.wait(X.vs.h, ck(f"st3_{m}"))
            S_.op("activation", out=cols3[:, 6:7], in_=cols3[:, 5:6], func=AF.Ln, bias=X.fconst[:, 0:1])
            S_.op("activation", out=cols3[:, 7:8], in_=cols3[:, 6:7], func=AF.Exp, scale=-0.5)
            S_.ck(f"rs3_{m}")
            for b in range(2):
                S_.wait(s3_tp.h, 2 * m + b + 1)
                if m >= 1 and b == 0:
                    S_.wait(s3_xs.h, 16 * m)
                S_.op("copy", out=x2st[:, b * 1024:(b + 1) * 1024], in_=ps_t3[:, 0:1024])
                S_.ck(f"te3_{m}_{b}")
        S_.wait(s3_gt.h, 1)
        S_.op("copy", out=X.gcol[0:1, 1:2], in_=ps_v[0:1, 0:1])
        S_.ck("ge4_0")
        S_.wait(s3_gt.h, 2)
        S_.op("copy", out=X.gcol[0:1, 2:3], in_=ps_v[0:1, 0:1])
        S_.ck("ge4_1")

    @block.sync
    def _(sy):
        sy.wait_ge(s2_spl[0].h, s2_spl[0].n)
        sy.wait_ge(s2_spl[1].h, s2_spl[1].n)
        def store(m):
            sy.wait_ge(X.ss.h, ck(f"te3_{m}_1"))
            sy.dma_start(out=X.scr_x2t[:, m * 128:(m + 1) * 128].rearrange("(k p) s -> p k s", p=128),
                         in_=x2st[:].rearrange("p (k s) -> p k s", s=128)).then_inc(s3_xs.h, 16)
            s3_xs.inc(16)
            sy.wait_ge(X.vs.h, ck(f"x2_{m}"))
            sy.dma_start(out=X.scr_out1[m * 128:(m + 1) * 128, :], in_=o1b[m % 2][:]).then_inc(s3_os[m % 2].h, 16)
            s3_os[m % 2].inc(16)
        for m in range(NM):
            if m >= 2:
                sy.wait_ge(s3_mm.h, m - 1)
            sy.dma_start(out=ygst[m % 2][:].rearrange("p (k s) -> p k s", k=32),
                         in_=X.scr_yg[:, m * 128:(m + 1) * 128].rearrange("(k p) s -> p k s", p=128)
                         ).then_inc(s3_yl[m % 2].h, 16)
            s3_yl[m % 2].inc(16)
            if m >= 2:
                sy.wait_ge(X.vs.h, ck(f"o1_{m-2}"))
            sy.dma_start(out=xres[m % 2][:], in_=X.x_d[HALO + m * 128: HALO + (m + 1) * 128, :]
                         ).then_inc(s3_xl[m % 2].h, 16)
            s3_xl[m % 2].inc(16)
            if m >= 1:
                store(m - 1)
        store(NM - 1)
        sy.wait_ge(X.ss.h, ck("ge4_0"))
        sy.dma_start(out=X.scr_g[0:1, 0:1], in_=X.gcol[0:1, 1:2]).then_inc(s3_gw.h, 16)
        s3_gw.inc(16)
        sy.wait_ge(s3_gw.h, s3_gw.n)
        sy.dma_start(out=X.gcol[:, 0:1], in_=X.scr_g[0:1, 0:1].partition_broadcast(128)
                     ).then_inc(s3_gb.h, 16)
        s3_gb.inc(16)
        sy.wait_ge(X.ss.h, ck("ge4_1"))
        sy.dma_start(out=X.scr_g[0:1, 1:2], in_=X.gcol[0:1, 2:3]).then_inc(s3_gw.h, 16)
        s3_gw.inc(16)
        sy.wait_ge(s3_gw.h, s3_gw.n)
        sy.wait_ge(X.vs.h, ck("t4_0"))
        sy.dma_start(out=X.gcol[:, 0:1], in_=X.scr_g[0:1, 1:2].partition_broadcast(128)
                     ).then_inc(s3_gb.h, 16)
        s3_gb.inc(16)

    @block.vector
    def _(v):
        V = X.VC(v)
        def chain(m):
            V.wait(s3_mm.h, m + 1)
            V.wait(s3_xl[m % 2].h, 16 * (m // 2 + 1))
            if m >= 2:
                V.wait(s3_os[m % 2].h, 16 * (m // 2))
                V.wait(X.ss.h, ck(f"sq3_{m-1}"))
            V.wait(X.ss.h, ck("rce"))
            V.op("scalar_tensor_tensor", out=o1b[m % 2][:], in0=ps_o[:],
                 scalar=rcol[:, m:m + 1], in1=xres[m % 2][:], op0=ALU.mult, op1=ALU.add)
            V.ck(f"o1_{m}")
            V.op("reduce_sum", out=cols3[:, 0:1], in_=o1b[m % 2][:], axis=AX.X)
            V.wait(X.ss.h, ck(f"sq3_{m}"))
            V.op("tensor_scalar_mul", out=cols3[:, 2:3], in0=cols3[:, 0:1], scalar1=1.0 / E)
            V.op("tensor_scalar_mul", out=cols3[:, 3:4], in0=cols3[:, 1:2], scalar1=1.0 / E)
            V.op("tensor_mul", out=cols3[:, 4:5], in0=cols3[:, 2:3], in1=cols3[:, 2:3])
            V.op("tensor_sub", out=cols3[:, 5:6], in0=cols3[:, 3:4], in1=cols3[:, 4:5])
            V.ck(f"st3_{m}")
            V.wait(X.ss.h, ck(f"rs3_{m}"))
            V.op("scalar_tensor_tensor", out=cols3[:, 8:9], in0=cols3[:, 2:3], scalar=-1.0,
                 in1=cols3[:, 7:8], op0=ALU.mult, op1=ALU.mult)
            if m >= 2:
                V.wait(s3_tp.h, 2 * (m - 1))
            V.op("tensor_scalar", out=x2b[m % 2][:], in0=o1b[m % 2][:],
                 scalar1=cols3[:, 7:8], scalar2=cols3[:, 8:9],
                 op0=ALU.mult, op1=ALU.add)
            V.ck(f"x2_{m}")
        p1i = [0]
        def p1drain(upto):
            while p1i[0] < min(upto, NS3):
                i = p1i[0]
                wd, r, c0, nc_, pc, accf = p1all[i]
                V.wait(s3_p1[i % 2].h, 16 * (i // 2 + 1))
                V.op("reduce_sum", out=X.parts[:, (51 if accf else pc):(52 if accf else pc + 1)],
                     in_=wp1[i % 2][:, 0:nc_], axis=AX.X, apply_absolute_value=True)
                if accf:
                    V.op("tensor_add", out=X.parts[:, pc:pc + 1], in0=X.parts[:, pc:pc + 1],
                         in1=X.parts[:, 51:52])
                V.ck(f"p4_{i}")
                p1i[0] += 1
                if i == 47:
                    V.op("reduce_sum", out=X.parts[:, 48:49], in_=X.parts[:, 0:16], axis=AX.X)
                    V.ck("gv4_0")
                elif i == 95:
                    V.op("reduce_sum", out=X.parts[:, 49:50], in_=X.parts[:, 16:32], axis=AX.X)
                    V.ck("gv4_1")
        for q in range(8):
            V.wait(s3_w[q].h, 16)
            V.op("tensor_tensor", out=wout_sb[:, q * 8192:(q + 1) * 8192].rearrange("p (k e) -> p k e", k=4),
                 in0=wout_sb[:, q * 8192:(q + 1) * 8192].rearrange("p (k e) -> p k e", k=4),
                 in1=X.normw_sb[:, q * 4:(q + 1) * 4].rearrange("p (k o) -> p k o", o=1)
                 .to_broadcast([128, 4, 2048]),
                 op=ALU.mult)
            V.ck(f"wn{q}")
        for m in range(NM):
            if m >= 1:
                chain(m - 1)
            p1drain((m + 1) * 6)
        chain(NM - 1)
        p1drain(NS3)
        V.wait(s3_gb.h, 16)
        V.op("tensor_scalar", out=X.tpos[:, 0:1], in0=X.gcol[:, 0:1], scalar1=0.5 / NW,
             scalar2=0.5e-8, op0=ALU.mult, op1=ALU.add)
        V.op("tensor_scalar_mul", out=X.tpos[:, 1:2], in0=X.tpos[:, 0:1], scalar1=-1.0)
        V.ck("t4_0")
        V.wait(s3_gb.h, 32)
        V.op("tensor_scalar", out=X.tpos[:, 2:3], in0=X.gcol[:, 0:1], scalar1=0.5 / NW,
             scalar2=0.5e-8, op0=ALU.mult, op1=ALU.add)
        V.op("tensor_scalar_mul", out=X.tpos[:, 3:4], in0=X.tpos[:, 2:3], scalar1=-1.0)
        V.ck("t4_1")

    X.final.append((s3_os[0], s3_os[0].n))
    X.final.append((s3_os[1], s3_os[1].n))
    X.final.append((s3_xs, 16 * NM))


def _stage4(X):
    block = X.block
    ck = X.ck
    nc = X.nc
    p1all = X.p1all
    w3p1 = p1all[96:140]
    MS = 20
    NG = 11

    with ExitStack() as f4b:
        fsb = lambda name, shp, dt: f4b.enter_context(nc.sbuf_tensor(name, shp, dt))
        fps = lambda name, shp, dt: f4b.enter_context(nc.psum_tensor(name, shp, dt))
        x2T = fsb("x2T", [128, NKT * 2048], BF16)
        qgrp = [[fsb(f"qgrp{mi}_{i}", [128, NKT * 512], BF16) for i in range(2)]
                for mi in range(2)]
        sgb = fsb("sgb", [128, 2048], BF16)
        hidb = [fsb(f"hidb{i}", [128, 2048], BF16) for i in range(2)]
        wq2 = [fsb(f"wq2_{i}", [128, 512], F32) for i in range(4)]
        qt2 = fsb("qt2", [128, 512], BF16)
        wq3 = [fsb(f"wq3_{i}", [128, 2048], F32) for i in range(2)]
        qb3 = [fsb(f"qb3_{i}", [128, 2048], BF16) for i in range(2)]
        qt3 = fsb("qt3", [128, 2048], BF16)
        grow = fsb("grow", [1, 128], F32)
        ps_x1 = fps("ps_x1", [128, 2048], F32)
        ps_x2 = fps("ps_x2", [128, 2048], F32)

        s4_x2 = X.sem("s4_x2")
        s4_wl = [X.sem(f"s4_wl{i}") for i in range(4)]
        s4_mmb = X.sem("s4_mmb")
        s4_hs = [X.sem("s4_hs0"), X.sem("s4_hs1")]
        s4_p3l = [X.sem("s4_p3l0"), X.sem("s4_p3l1")]
        s4_w3l = [X.sem("s4_w3l0"), X.sem("s4_w3l1")]
        s4_q3s = [X.sem("s4_q3s0"), X.sem("s4_q3s1")]
        s4_gw = X.sem("s4_gw"); s4_gr = X.sem("s4_gr"); s4_gb = X.sem("s4_gb")
        X.s4_hs = s4_hs
        X.s4_q3s = s4_q3s
        s3_os, s3_xs = X.s3_os, X.s3_xs

        def w3blocks(m):
            if m < MS:
                return []
            return [b for b in (2 * (m - MS), 2 * (m - MS) + 1) if b < NFT]

        @block.sync
        def _(sy):
            sy.wait_ge(s3_os[0].h, s3_os[0].n)
            sy.wait_ge(s3_os[1].h, s3_os[1].n)
            sy.wait_ge(s3_xs.h, 256)
            sy.wait_ge(X.vs.h, ck("t4_1"))
            sy.wait_ge(X.ss.h, ck("ge4_1"))
            for q in range(8):
                sy.dma_start(out=x2T[:, q * 4096:(q + 1) * 4096].rearrange("p (k s) -> p k s", k=2),
                             in_=X.scr_x2t[q * 256:(q + 1) * 256, :].rearrange("(k p) s -> p k s", p=128)
                             ).then_inc(s4_x2.h, 16)
                s4_x2.inc(16)
            p3q = [0]
            def p3issue(upto):
                while p3q[0] < min(upto, NFT):
                    i = p3q[0]
                    wd, r, c0, nc_, pc, accf = w3p1[i]
                    if i >= 2:
                        sy.wait_ge(X.ss.h, ck(f"p4w3_{i-2}"))
                    sy.dma_start(out=wq3[i % 2][:], in_=wd[r * 128:(r + 1) * 128, :]
                                 ).then_inc(s4_p3l[i % 2].h, 16)
                    s4_p3l[i % 2].inc(16)
                    p3q[0] += 1
            for c in range(NG + 1):
                for j in (range(32) if c < NG else []):
                    ci = c * 32 + j
                    mi_, r = j // 16, j % 16
                    wd = X.w1_d if mi_ == 0 else X.w2_d
                    if ci >= 4:
                        sy.wait_ge(X.vs.h, ck(f"qg_{ci-4}"))
                    sy.dma_start(out=wq2[ci % 4][:], in_=wd[r * 128:(r + 1) * 128, c * 512:(c + 1) * 512]
                                 ).then_inc(s4_wl[ci % 4].h, 16)
                    s4_wl[ci % 4].inc(16)
                if c == 0:
                    continue
                for m in range(4 * (c - 1), min(4 * (c - 1) + 4, NFT)):
                    p3issue((m + 1) * 3)
                    if m == 16:
                        sy.wait_ge(X.vs.h, ck("gvp"))
                        sy.dma_start(out=X.scr_gt[0:1, :].rearrange("o p -> p o", p=128),
                                     in_=X.parts[:, 50:51]).then_inc(s4_gw.h, 16)
                        s4_gw.inc(16)
                        sy.wait_ge(s4_gw.h, s4_gw.n)
                        sy.dma_start(out=grow[0:1, :], in_=X.scr_gt[0:1, :]).then_inc(s4_gr.h, 16)
                        s4_gr.inc(16)
                        sy.wait_ge(X.vs.h, ck("gvw3"))
                        sy.dma_start(out=X.scr_g[0:1, 2:3], in_=X.gcol[0:1, 3:4]).then_inc(s4_gw.h, 16)
                        s4_gw.inc(16)
                        sy.wait_ge(s4_gw.h, s4_gw.n)
                        sy.dma_start(out=X.gcol[:, 0:1], in_=X.scr_g[0:1, 2:3].partition_broadcast(128)
                                     ).then_inc(s4_gb.h, 16)
                        s4_gb.inc(16)
                    for b in w3blocks(m):
                        if b < 2:
                            sy.wait_ge(X.ss.h, ck("p4w3_43"))
                        else:
                            sy.wait_ge(X.vs.h, ck(f"q3_{b-2}"))
                        sy.dma_start(out=wq3[b % 2][:], in_=X.w3_d[b * 128:(b + 1) * 128, :]
                                     ).then_inc(s4_w3l[b % 2].h, 16)
                        s4_w3l[b % 2].inc(16)
                        if b >= 1:
                            sy.wait_ge(X.vs.h, ck(f"q3_{b-1}"))
                            sy.dma_start(out=X.scr_q3[(b - 1) * 128:b * 128, :], in_=qb3[(b - 1) % 2][:]
                                         ).then_inc(s4_q3s[(b - 1) % 2].h, 16)
                            s4_q3s[(b - 1) % 2].inc(16)
                    if m >= 1:
                        sy.wait_ge(X.vs.h, ck(f"hd{m-1}"))
                        sy.dma_start(out=X.scr_hid[(m - 1) * 128: m * 128, :], in_=hidb[(m - 1) % 2][:]
                                     ).then_inc(s4_hs[(m - 1) % 2].h, 16)
                        s4_hs[(m - 1) % 2].inc(16)
            sy.wait_ge(X.vs.h, ck(f"q3_{NFT-1}"))
            sy.dma_start(out=X.scr_q3[(NFT - 1) * 128:NFT * 128, :], in_=qb3[(NFT - 1) % 2][:]
                         ).then_inc(s4_q3s[(NFT - 1) % 2].h, 16)
            s4_q3s[(NFT - 1) % 2].inc(16)
            sy.wait_ge(X.vs.h, ck(f"hd{NFT-1}"))
            sy.dma_start(out=X.scr_hid[(NFT - 1) * 128: NFT * 128, :], in_=hidb[(NFT - 1) % 2][:]
                         ).then_inc(s4_hs[(NFT - 1) % 2].h, 16)
            s4_hs[(NFT - 1) % 2].inc(16)

        @block.vector
        def _(v):
            V = X.VC(v)
            for c in range(NG + 1):
                for j in (range(32) if c < NG else []):
                    ci = c * 32 + j
                    mi_, r = j // 16, j % 16
                    V.wait(s4_wl[ci % 4].h, 16 * (ci // 4 + 1))
                    if c >= 2 and j == 0:
                        V.wait(s4_mmb.h, 2 * (4 * (c - 2) + 4))
                    dst = qgrp[mi_][c % 2][:, r * 512:(r + 1) * 512]
                    tp = X.tpos[:, 2 * mi_:2 * mi_ + 1]
                    tn = X.tpos[:, 2 * mi_ + 1:2 * mi_ + 2]
                    V.op("tensor_single_scalar", out=dst, in_=wq2[ci % 4][:], scalar=tp, op=ALU.is_gt)
                    V.op("tensor_single_scalar", out=qt2[:], in_=wq2[ci % 4][:], scalar=tn, op=ALU.is_lt)
                    V.op("tensor_sub", out=dst, in0=dst, in1=qt2[:])
                    V.ck(f"qg_{ci}")
                if c == 0:
                    continue
                for m in range(4 * (c - 1), min(4 * (c - 1) + 4, NFT)):
                    if m == 15:
                        V.wait(X.ss.h, ck("p4w3_43"))
                        V.op("reduce_sum", out=X.parts[:, 50:51], in_=X.parts[:, 0:44], axis=AX.X)
                        V.ck("gvp")
                    if m == 16:
                        V.wait(s4_gr.h, 16)
                        V.op("reduce_sum", out=X.gcol[0:1, 3:4], in_=grow[0:1, :], axis=AX.X)
                        V.ck("gvw3")
                    if m == MS:
                        V.wait(s4_gb.h, 16)
                        V.op("tensor_scalar", out=X.tpos[:, 4:5], in0=X.gcol[:, 0:1], scalar1=0.5 / NW,
                             scalar2=0.5e-8, op0=ALU.mult, op1=ALU.add)
                        V.op("tensor_scalar_mul", out=X.tpos[:, 5:6], in0=X.tpos[:, 4:5], scalar1=-1.0)
                        V.ck("t4_2")
                    V.wait(X.ss.h, ck(f"sg{m}"))
                    V.wait(s4_mmb.h, 2 * m + 2)
                    if m >= 2:
                        V.wait(s4_hs[m % 2].h, 16 * (m // 2))
                    V.op("tensor_mul", out=hidb[m % 2][:], in0=sgb[:], in1=ps_x2[:])
                    V.ck(f"hd{m}")
                    for b in w3blocks(m):
                        V.wait(s4_w3l[b % 2].h, 16 * (b // 2 + 1))
                        if b >= 2:
                            V.wait(s4_q3s[b % 2].h, 16 * ((b - 2) // 2 + 1))
                        V.op("tensor_single_scalar", out=qb3[b % 2][:], in_=wq3[b % 2][:],
                             scalar=X.tpos[:, 4:5], op=ALU.is_gt)
                        V.op("tensor_single_scalar", out=qt3[:], in_=wq3[b % 2][:],
                             scalar=X.tpos[:, 5:6], op=ALU.is_lt)
                        V.op("tensor_sub", out=qb3[b % 2][:], in0=qb3[b % 2][:], in1=qt3[:])
                        V.ck(f"q3_{b}")

        @block.tensor
        def _(t):
            t.wait_ge(s4_x2.h, 128)
            for m in range(NFT):
                c = m // 4
                t.wait_ge(X.vs.h, ck(f"qg_{c * 32 + 31}"))
                if m >= 1:
                    t.wait_ge(X.ss.h, ck(f"sg{m-1}"))
                lof = (m % 4) * 128
                mi = None
                for nb in range(4):
                    for k in range(NKT):
                        mi = t.matmul(out=ps_x1[:, nb * 512:(nb + 1) * 512],
                                      lhsT=qgrp[0][c % 2][:, k * 512 + lof: k * 512 + lof + 128],
                                      rhs=x2T[:, k * 2048 + nb * 512: k * 2048 + (nb + 1) * 512],
                                      start=(k == 0), stop=(k == NKT - 1))
                mi.then_inc(s4_mmb.h, 1)
                if m >= 1:
                    t.wait_ge(X.vs.h, ck(f"hd{m-1}"))
                mi = None
                for nb in range(4):
                    for k in range(NKT):
                        mi = t.matmul(out=ps_x2[:, nb * 512:(nb + 1) * 512],
                                      lhsT=qgrp[1][c % 2][:, k * 512 + lof: k * 512 + lof + 128],
                                      rhs=x2T[:, k * 2048 + nb * 512: k * 2048 + (nb + 1) * 512],
                                      start=(k == 0), stop=(k == NKT - 1))
                mi.then_inc(s4_mmb.h, 1); s4_mmb.inc(2)

        @block.scalar
        def _(sc):
            S_ = X.SC(sc)
            p3i = [0]
            def p3drain(upto):
                while p3i[0] < min(upto, NFT):
                    i = p3i[0]
                    wd, r, c0, nc_, pc, accf = w3p1[i]
                    S_.wait(s4_p3l[i % 2].h, 16 * (i // 2 + 1))
                    S_.op("activation", out=wq3[i % 2][:], in_=wq3[i % 2][:], func=AF.Abs,
                          accum_out=X.parts[:, pc:pc + 1])
                    S_.ck(f"p4w3_{i}")
                    p3i[0] += 1
            for m in range(NFT):
                S_.wait(s4_mmb.h, 2 * m + 1)
                if m >= 1:
                    S_.wait(X.vs.h, ck(f"hd{m-1}"))
                S_.op("activation", out=sgb[:], in_=ps_x1[:], func=AF.Sigmoid)
                S_.ck(f"sg{m}")
                p3drain((m + 1) * 3)

    with ExitStack() as f4c:
        fsb = lambda name, shp, dt: f4c.enter_context(nc.sbuf_tensor(name, shp, dt))
        fps = lambda name, shp, dt: f4c.enter_context(nc.psum_tensor(name, shp, dt))
        q3h = fsb("q3h", [128, NFT * 1024], BF16)
        hist = [fsb(f"hist{i}", [128, NFT * 128], BF16) for i in range(2)]
        o1t = [fsb(f"o1t{i}", [128, 1024], BF16) for i in range(2)]
        outf = [fsb(f"outf{i}", [128, 1024], F32) for i in range(2)]
        ps_o3 = [fps(f"ps_o3{i}", [128, 1024], F32) for i in range(2)]

        s4_q3 = X.sem("s4_q3")
        s4_hl = [X.sem("s4_hl0"), X.sem("s4_hl1")]
        s4_o1l = [X.sem("s4_o1l0"), X.sem("s4_o1l1")]
        s4_mmc = X.sem("s4_mmc")
        s_out = [X.sem("s_out0"), X.sem("s_out1")]
        s4_hs = X.s4_hs

        @block.sync
        def _(sy):
            sy.wait_ge(s4_hs[0].h, s4_hs[0].n)
            sy.wait_ge(s4_hs[1].h, s4_hs[1].n)
            sy.wait_ge(X.s4_q3s[0].h, X.s4_q3s[0].n)
            sy.wait_ge(X.s4_q3s[1].h, X.s4_q3s[1].n)
            for eh in range(2):
                if eh == 1:
                    sy.wait_ge(s4_mmc.h, 16)
                for q in range(4):
                    sy.dma_start(out=q3h[:, q * 11 * 1024:(q + 1) * 11 * 1024].rearrange("p (k e) -> p k e", k=11),
                                 in_=X.scr_q3[q * 1408:(q + 1) * 1408, eh * 1024:(eh + 1) * 1024]
                                 .rearrange("(k p) e -> p k e", p=128)).then_inc(s4_q3.h, 16)
                    s4_q3.inc(16)
                for m in range(16):
                    gi = eh * 16 + m
                    if gi >= 2:
                        sy.wait_ge(s4_mmc.h, gi - 1)
                    sy.dma_start(out=hist[gi % 2][:].rearrange("p (k s) -> p k s", k=NFT),
                                 in_=X.scr_hid[:, m * 128:(m + 1) * 128].rearrange("(k p) s -> p k s", p=128)
                                 ).then_inc(s4_hl[gi % 2].h, 16)
                    s4_hl[gi % 2].inc(16)
                    if gi >= 2:
                        sy.wait_ge(X.vs.h, ck(f"of{gi-2}"))
                    sy.dma_start(out=o1t[gi % 2][:],
                                 in_=X.scr_out1[m * 128:(m + 1) * 128, eh * 1024:(eh + 1) * 1024]
                                 ).then_inc(s4_o1l[gi % 2].h, 16)
                    s4_o1l[gi % 2].inc(16)

        @block.tensor
        def _(t):
            for eh in range(2):
                t.wait_ge(s4_q3.h, 64 * (eh + 1))
                for m in range(16):
                    gi = eh * 16 + m
                    t.wait_ge(s4_hl[gi % 2].h, 16 * (gi // 2 + 1))
                    if gi >= 2:
                        t.wait_ge(X.vs.h, ck(f"of{gi-2}"))
                    mi = None
                    for nb in range(2):
                        for k in range(NFT):
                            mi = t.matmul(out=ps_o3[gi % 2][:, nb * 512:(nb + 1) * 512],
                                          lhsT=hist[gi % 2][:, k * 128:(k + 1) * 128],
                                          rhs=q3h[:, k * 1024 + nb * 512: k * 1024 + (nb + 1) * 512],
                                          start=(k == 0), stop=(k == NFT - 1))
                    mi.then_inc(s4_mmc.h, 1); s4_mmc.inc()

        @block.vector
        def _(v):
            V = X.VC(v)
            for eh in range(2):
                for m in range(16):
                    gi = eh * 16 + m
                    V.wait(s4_mmc.h, gi + 1)
                    V.wait(s4_o1l[gi % 2].h, 16 * (gi // 2 + 1))
                    if gi >= 2:
                        V.wait(s_out[gi % 2].h, 16 * (gi // 2))
                    V.op("tensor_add", out=outf[gi % 2][:], in0=ps_o3[gi % 2][:], in1=o1t[gi % 2][:])
                    V.ck(f"of{gi}")

        @block.gpsimd
        def _(g):
            for eh in range(2):
                for m in range(16):
                    gi = eh * 16 + m
                    g.wait_ge(X.vs.h, ck(f"of{gi}"))
                    g.dma_start(out=X.y_d[m * 128:(m + 1) * 128, eh * 1024:(eh + 1) * 1024],
                                in_=outf[gi % 2][:]).then_inc(s_out[gi % 2].h, 16)
                    s_out[gi % 2].inc(16)
            g.wait_ge(s_out[0].h, s_out[0].n)
            g.wait_ge(s_out[1].h, s_out[1].n)

        X.out_written = True
        X.final = [(s_out[0], s_out[0].n), (s_out[1], s_out[1].n)]


def _finish(X):
    block = X.block
    nc = X.nc
    if X.out_written:
        return
    with nc.sbuf_tensor("zfin", [128, E], F32) as zf, \
         nc.semaphore("finv") as finv, nc.semaphore("fin") as fin:
        @block.vector
        def _(v):
            v.wait_ge(X.vs.h, X.vs.n)
            v.memset(zf[:], 0.0).then_inc(finv, 1)

        @block.gpsimd
        def _(g):
            g.wait_ge(finv, 1)
            for ent in X.final:
                if ent[0] == "ck":
                    g.wait_ge(ent[1].h, X.ck(ent[2]))
                else:
                    g.wait_ge(ent[0].h, ent[1])
            for i in range(S // 128):
                g.dma_start(out=X.y_d[i * 128:(i + 1) * 128, :], in_=zf[:]).then_inc(fin, 16)
            g.wait_ge(fin, 16 * (S // 128))



def _host_inputs(inputs):
    x = np.asarray(inputs["L0_attn_input_states"], dtype=np.float32).reshape(16384, 2048)
    win0 = np.asarray(inputs["L0_attn_W_in"], dtype=np.float32)
    winp = np.concatenate([win0, np.zeros((E, 73 * 128 - PROJ), np.float32)], axis=1)
    win = np.ascontiguousarray(
        winp.reshape(NKT, 128, 73, 128).transpose(2, 1, 0, 3).reshape(73, 128, NKT * 128))
    convw = np.ascontiguousarray(
        np.asarray(inputs["L0_attn_conv_w"], dtype=np.float32).reshape(40, 128, 4)
        .transpose(1, 0, 2).reshape(128, 160))
    convb = np.ascontiguousarray(
        np.asarray(inputs["L0_attn_conv_b"], dtype=np.float32).reshape(40, 128).T)
    dtb = np.ascontiguousarray(np.asarray(inputs["L0_attn_dt_bias"], dtype=np.float32).reshape(NH, 1))
    alog = np.ascontiguousarray(np.asarray(inputs["L0_attn_A_log"], dtype=np.float32).reshape(NH, 1))
    drep = np.ascontiguousarray(
        np.repeat(np.asarray(inputs["L0_attn_D"], dtype=np.float32), 64).reshape(32, 128).T)
    normw = np.ascontiguousarray(
        np.asarray(inputs["L0_attn_norm_w"], dtype=np.float32).reshape(32, 128).T)
    wout0 = np.asarray(inputs["L0_attn_W_out"], dtype=np.float32)
    wout = np.ascontiguousarray(
        wout0.reshape(32, 128, E).transpose(1, 0, 2).reshape(128, 32 * E))
    w1 = np.ascontiguousarray(np.asarray(inputs["L0_ffn_w1"], dtype=np.float32).T)
    w2 = np.ascontiguousarray(np.asarray(inputs["L0_ffn_w2"], dtype=np.float32).T)
    w3 = np.ascontiguousarray(np.asarray(inputs["L0_ffn_w3"], dtype=np.float32).T)
    identb = np.eye(128, dtype=ml_dtypes.bfloat16)
    identf = np.eye(128, dtype=np.float32)
    i0 = np.arange(128)[:, None]
    l0 = np.arange(256)[None, :]
    mask = np.concatenate([(l0 >= i0).astype(np.float32),
                           (l0 >= i0 + 128).astype(np.float32)], axis=0)
    in_maps = []
    for c in range(8):
        r0 = c * S
        if c == 0:
            xc = np.concatenate([np.zeros((HALO, E), np.float32), x[0:S, :]], axis=0)
        else:
            xc = x[r0 - HALO:r0 + S, :]
        in_maps.append(dict(
            x=np.ascontiguousarray(xc), w_in=win, conv_w=convw, conv_b=convb,
            dt_bias=dtb, a_log=alog, d_rep=drep, norm_w=normw, w_out=wout,
            w1=w1, w2=w2, w3=w3, identb=identb, identf=identf, mask=mask,
        ))
    return in_maps


def kernel(**inputs):
    in_maps = _host_inputs(inputs)
    nc = build_nc()
    res = run_bass_kernel_spmd(nc, in_maps, core_ids=list(range(8)))
    out = np.concatenate([np.asarray(r["y"]) for r in res.results], axis=0)
    return out.reshape(1, 16384, 2048).astype(np.float32)
